# version 01 (final)
# baseline (speedup 1.0000x reference)
import numpy as np
from contextlib import ExitStack

import concourse.bass as bass
import concourse.mybir as mybir
from concourse.bass_utils import run_bass_kernel_spmd

F32 = mybir.dt.float32
F16 = mybir.dt.float16
AF = mybir.ActivationFunctionType
ALU = mybir.AluOpType

DM, DI, NS, DTR, L = 1024, 2048, 16, 64, 2048
CH, CW = 4, 512
NB, HD, DH, NE, FF = 8, 16, 64, 8, 4096
TOK, SL = 2048, 256

DEBUG_TAPS = ()

ENGINES = ("sync", "tensor", "vector", "scalar", "gpsimd")
SEMS = ("dS", "dG", "T", "V", "S", "G")
DMA_SEMS = ("dS", "dG")


class Ev:
    __slots__ = ("sem", "val")

    def __init__(self, sem, val):
        self.sem = sem
        self.val = val


class Prog:
    def __init__(self):
        self.ops = {e: [] for e in ENGINES}
        self.n = {s: 0 for s in SEMS}

    def add(self, eng, fn, waits=(), mark=None):
        ev = None
        if mark is not None:
            self.n[mark] += 16 if mark in DMA_SEMS else 1
            ev = Ev(mark, self.n[mark])
        self.ops[eng].append((fn, tuple(w for w in waits if w is not None), ev))
        return ev

    def bar(self):
        return [Ev(s, self.n[s]) for s in SEMS if self.n[s] > 0]

    def emit(self, block, sems):
        def run(name):
            def body(eng):
                hw = {s: 0 for s in SEMS}
                for fn, waits, ev in self.ops[name]:
                    for w in waits:
                        if w.val > hw[w.sem]:
                            eng.wait_ge(sems[w.sem], w.val)
                            hw[w.sem] = w.val
                    instr = fn(eng)
                    if ev is not None:
                        instr.then_inc(sems[ev.sem],
                                       16 if ev.sem in DMA_SEMS else 1)
            return body
        block.sync(run("sync"))
        block.tensor(run("tensor"))
        block.vector(run("vector"))
        block.scalar(run("scalar"))
        block.gpsimd(run("gpsimd"))


class Arena:

    _seq = [0]

    def __init__(self, nc, size):
        Arena._seq[0] += 1
        self.t = nc.alloc_sbuf_tensor(f"arena{Arena._seq[0]}", [128, size],
                                      mybir.dt.uint8)
        self.size = size
        self.freel = [(0, size)]
        self.live = {}

    def al(self, name, shape, dtype):
        esz = mybir.dt.size(dtype)
        nb = int(np.prod(shape[1:])) * esz
        nba = (nb + 63) & ~63
        for i, (o, s) in enumerate(self.freel):
            if s >= nba:
                if s == nba:
                    self.freel.pop(i)
                else:
                    self.freel[i] = (o + nba, s - nba)
                ap = self.t.ap()[:shape[0], o:o + nb].bitcast(dtype)
                if len(shape) > 2:
                    dims = {f"d{k}": shape[k] for k in range(1, len(shape))}
                    pat = ("p (" + " ".join(dims) + ") -> p " + " ".join(dims))
                    ap = ap.rearrange(pat, **{k: v for k, v in list(dims.items())[:-1]})
                self.live[name] = (o, nba)
                return ap
        used = self.size - sum(s for _, s in self.freel)
        raise MemoryError(f"arena OOM for {name} ({nba}B; used {used}/{self.size})")

    def fr(self, *names):
        for name in names:
            o, s = self.live.pop(name)
            self.freel.append((o, s))
        self.freel.sort()
        out = []
        for o, s in self.freel:
            if out and out[-1][0] + out[-1][1] == o:
                out[-1] = (out[-1][0], out[-1][1] + s)
            else:
                out.append((o, s))
        self.freel = [tuple(x) for x in out]


def t128(ap):
    return ap.rearrange("(t p) m -> t p m", p=128).transpose([1, 0, 2])




class _Stop(Exception):
    pass


def _finish(nc, ctx, P, block_sems_taps):
    sems, taps = block_sems_taps
    P.add("sync", lambda s: s.nop(), waits=P.bar())
    with nc.Block() as block:
        P.emit(block, sems)
    nc._keepalive = ctx
    return nc, taps


def build(stage="full", debug_taps=()):
    nc = bass.Bass("TRN2", target_bir_lowering=False, debug=False,
                   detect_race_conditions=False)
    nc.cache_partition_id()
    ctx = ExitStack()
    cc = ctx.enter_context
    P = Prog()
    STAGES = ["p1", "branchf", "branchb", "p4", "ag", "qkv", "attn", "ln1",
              "gate", "moe", "full"]
    sidx = STAGES.index(stage)

    def want(st):
        return sidx >= STAGES.index(st)

    def din(name, shape, dt=F32):
        return nc.dram_tensor(name, shape, dt, kind="ExternalInput").ap()

    xT = din("xT", [DM, L])
    lng = din("lng", [128, 8]); lnb = din("lnb", [128, 8])
    br_in = {}
    for p in ("f", "b"):
        br_in[p] = dict(
            inwT=din(f"inwT_{p}", [DM, 2 * DI]),
            convd=din(f"convd_{p}", [128, 16, 4, 128]),
            convb=din(f"convb_{p}", [128, 16]),
            xprojT=din(f"xprojT_{p}", [128, 16, 96]),
            dtwT=din(f"dtwT_{p}", [DTR, DI]),
            dtb=din(f"dtb_{p}", [128, 16]),
            alog=din(f"alog_{p}", [128, 16, NS]),
            dvec=din(f"dvec_{p}", [128, 16]),
            outwT=din(f"outwT_{p}", [DI, DM]),
        )
    projT = din("projT", [2 * DM, DM])
    projb = din("projb", [128, 8])
    inprojT = din("inprojT", [DM, 3 * DM])
    inprojb = din("inprojb", [128, 24])
    outw2T = din("outw2T", [DM, DM])
    outb2 = din("outb2", [128, 8])
    gateT = din("gateT", [DM, NE])
    gatebr = din("gatebr", [128, NE])
    w1t = din("w1t", [NE, 32, 8, 128, 128])
    b1h = din("b1h", [NE, 128, 32])
    w2t = din("w2t", [NE, 8, 32, 128, 128])
    b2h = din("b2h", [NE, 128, 8])
    ln1g = din("ln1g", [128, 8]); ln1b = din("ln1b", [128, 8])
    ln2g = din("ln2g", [128, 8]); ln2b = din("ln2b", [128, 8])
    ident_in = din("ident_in", [128, 128], F16)
    ones16_in = din("ones16", [128, 1], F16)
    onesrow_in = din("onesrow", [1, 128])
    headsel_in = din("headsel", [128, 8, HD], F16)
    headexp_in = din("headexp", [HD, 8, 128], F16)
    selBC_in = din("selBC", [96, 32, 128], F16)
    sel8_in = din("sel8", [8, NE, 128], F16)
    mrfill_in = din("mrfill", [128, NE])
    cst_in = din("cst", [128, 2])

    out_h = nc.dram_tensor("out", [DM, TOK], F32, kind="ExternalOutput").ap()
    taps = {}
    for t in debug_taps:
        shp = {
            "hn": [DM, L], "xr": [DI, L], "u": [DI, L], "dt": [DI, L],
            "dbc": [96, L], "yact": [DI, L], "g": [DI, L], "o_f": [DM, L],
            "o_b": [DM, L], "x1": [DM, L], "qkv": [3 * DM, TOK],
            "S": [16, 8 * SL], "ctx": [DM, TOK], "x2": [DM, TOK],
            "comb": [8, TOK], "moe": [DM, TOK],
            "crep2": [128, TOK], "h1b": [FF, 2 * CW],
        }[t]
        taps[t] = nc.dram_tensor("tap_" + t, shp, F16, kind="ExternalOutput").ap()

    def scr(name, shape, dt=F16):
        return nc.dram_tensor(name, shape, dt, kind="Internal").ap()

    hnsp = {"f": scr("hnspf", [8, 128, L]), "b": scr("hnspb", [8, 128, L])}
    szsp = scr("szsp", [16, 128, L])
    usp = scr("usp", [16, 128, L])
    dtsp = scr("dtsp", [16, 128, L])
    dusp = scr("dusp", [16, 128, L])
    osp = {"f": scr("ospf", [8, 128, L]), "b": scr("ospb", [8, 128, L])}
    y0sp = scr("y0sp", [8, 128, TOK])
    cc_in = scr("cc_in", [DM, L])
    cc_out = scr("cc_out", [NB, DM, L])

    sems = {s: cc(nc.semaphore(s)) for s in SEMS}
    pb = [cc(nc.psum_tensor(f"pb{i}", [128, CW], F32)) for i in range(8)]
    banklast = [None] * 8

    def mmg(bank, mmfn, waits, evac_eng, evacfn, evac_waits=()):
        w = list(waits)
        if banklast[bank] is not None:
            w.append(banklast[bank])
        mv = P.add("tensor", mmfn, waits=w, mark="T")
        ev = P.add(evac_eng, evacfn, waits=[mv, *evac_waits],
                   mark={"vector": "V", "scalar": "S"}[evac_eng])
        banklast[bank] = ev
        return ev

    g_lng = cc(nc.sbuf_tensor("g_lng", [128, 8], F32))
    g_lnb = cc(nc.sbuf_tensor("g_lnb", [128, 8], F32))
    g_projb = cc(nc.sbuf_tensor("g_projb", [128, 8], F32))
    g_ident = cc(nc.sbuf_tensor("g_ident", [128, 128], F16))
    g_ones16 = cc(nc.sbuf_tensor("g_ones16", [128, 1], F16))
    g_onesrow = cc(nc.sbuf_tensor("g_onesrow", [1, 128], F32))
    g_selBC = cc(nc.sbuf_tensor("g_selBC", [96, 32, 128], F16))
    g_conv = cc(nc.sbuf_tensor("g_conv", [128, 16, 4, 128], F16))
    g_convb = {p: cc(nc.sbuf_tensor(f"g_convb_{p}", [128, 16], F32)) for p in ("f", "b")}
    g_dtb = {p: cc(nc.sbuf_tensor(f"g_dtb_{p}", [128, 16], F32)) for p in ("f", "b")}
    g_dvec = {p: cc(nc.sbuf_tensor(f"g_dvec_{p}", [128, 16], F32)) for p in ("f", "b")}
    g_A = {p: cc(nc.sbuf_tensor(f"g_A_{p}", [128, 16, NS], F32)) for p in ("f", "b")}
    g_dtw = cc(nc.sbuf_tensor("g_dtw", [DTR, DI], F16))
    g_xproj = cc(nc.sbuf_tensor("g_xproj", [128, 16, 96], F16))
    g_inprojb = cc(nc.sbuf_tensor("g_inprojb", [128, 24], F32))
    g_outb2 = cc(nc.sbuf_tensor("g_outb2", [128, 8], F32))
    g_gatebr = cc(nc.sbuf_tensor("g_gatebr", [128, NE], F32))
    g_b1 = cc(nc.sbuf_tensor("g_b1", [128, NE, 32], F32))
    g_b2 = cc(nc.sbuf_tensor("g_b2", [128, NE, 8], F32))
    g_ln1g = cc(nc.sbuf_tensor("g_ln1g", [128, 8], F32))
    g_ln1b = cc(nc.sbuf_tensor("g_ln1b", [128, 8], F32))
    g_ln2g = cc(nc.sbuf_tensor("g_ln2g", [128, 8], F32))
    g_ln2b = cc(nc.sbuf_tensor("g_ln2b", [128, 8], F32))
    g_headsel = cc(nc.sbuf_tensor("g_headsel", [128, 8, HD], F16))
    g_headexp = cc(nc.sbuf_tensor("g_headexp", [HD, 8, 128], F16))
    g_sel8 = cc(nc.sbuf_tensor("g_sel8", [8, NE, 128], F16))
    g_mrfill = cc(nc.sbuf_tensor("g_mrfill", [128, NE], F32))
    g_cst = cc(nc.sbuf_tensor("g_cst", [128, 2], F32))

    loads = [
        (g_lng[:], lng), (g_lnb[:], lnb), (g_projb[:], projb),
        (g_onesrow[:], onesrow_in), (g_inprojb[:], inprojb),
        (g_outb2[:], outb2), (g_gatebr[:], gatebr),
        (g_ln1g[:], ln1g), (g_ln1b[:], ln1b), (g_ln2g[:], ln2g),
        (g_ln2b[:], ln2b), (g_mrfill[:], mrfill_in),
        (g_ident[:], ident_in), (g_ones16[:], ones16_in),
        (g_headsel[:], headsel_in), (g_headexp[:], headexp_in),
        (g_selBC[:], selBC_in), (g_sel8[:], sel8_in),
        (g_cst[:], cst_in),
    ]
    for e in range(NE):
        loads += [(g_b1[:, e, :], b1h[e]), (g_b2[:, e, :], b2h[e])]
    for p in ("f", "b"):
        loads += [(g_convb[p][:], br_in[p]["convb"]), (g_dtb[p][:], br_in[p]["dtb"]),
                  (g_dvec[p][:], br_in[p]["dvec"]), (g_A[p][:], br_in[p]["alog"])]
    P.add("sync", lambda s: (nc.partition_id((mybir.EngineType.SP,)),
                             s.nop())[-1])
    for dst, src in loads:
        P.add("sync", lambda s, dst=dst, src=src: s.dma_start(out=dst, in_=src),
              mark="dS")
    b0 = P.bar()
    ae = [P.add("scalar", lambda s, p=p: s.activation(
        out=g_A[p][:], in_=g_A[p][:], func=AF.Exp), waits=b0, mark="S")
        for p in ("f", "b")]
    negA = [P.add("vector", lambda v, p=p: v.tensor_scalar_mul(
        g_A[p][:], g_A[p][:], -1.0), waits=ae, mark="V") for p in ("f", "b")]

    AR = None
    for trysz in (170, 168, 166, 164, 160, 156, 152):
        try:
            AR = Arena(nc, trysz * 1024)
            break
        except AssertionError:
            continue
    assert AR is not None, "could not allocate arena"

    def make_ln(tag, with16):
        lnsq = AR.al(tag + "lnsq", [128, 8, CW], F16)
        lnrow = AR.al(tag + "lnrow", [1, 6 * CW], F32)
        lnrep = AR.al(tag + "lnrep", [128, 2, CW], F32)
        lnnrm = AR.al(tag + "lnnrm", [128, 8, CW], F16)
        ln16 = AR.al(tag + "ln16", [128, 8, CW], F16) if with16 else None
        names = [tag + n for n in ("lnsq", "lnrow", "lnrep", "lnnrm")]
        if with16:
            names.append(tag + "ln16")

        def ln_free():
            AR.fr(*names)

        def emit_ln(nd, xap, x_is_16, sink, pre=()):
            evs = []
            if x_is_16:
                s16 = xap
            else:
                for i in range(nd):
                    evs.append(P.add("vector", lambda v, i=i: v.tensor_copy(
                        out=ln16[:, i, :], in_=xap(i)), waits=pre, mark="V"))
                s16 = lambda i: ln16[:, i, :]
            for i in range(nd):
                evs.append(P.add("scalar", lambda s, i=i: s.activation(
                    out=lnsq[:, i, :], in_=xap(i), func=AF.Square),
                    waits=pre, mark="S"))

            def statf(t, nd=nd):
                for i in range(nd):
                    t.matmul(pb[6][:1, :], g_ones16[:], s16(i),
                             start=(i == 0), stop=(i == nd - 1))
                for i in range(nd):
                    v = t.matmul(pb[7][:1, :], g_ones16[:], lnsq[:, i, :],
                                 start=(i == 0), stop=(i == nd - 1))
                return v
            w = list(evs) + list(pre)
            if banklast[6] is not None:
                w.append(banklast[6])
            if banklast[7] is not None:
                w.append(banklast[7])
            mv = P.add("tensor", statf, waits=w, mark="T")
            r = [lnrow[:, k * CW:(k + 1) * CW] for k in range(6)]
            P.add("vector", lambda v: v.tensor_scalar_mul(
                r[2], pb[6][:1, :], 1.0 / (nd * 128)), waits=[mv])
            P.add("vector", lambda v: v.tensor_scalar_mul(
                r[1], pb[7][:1, :], 1.0 / (nd * 128)))
            P.add("vector", lambda v: v.tensor_mul(r[0], r[2], r[2]))
            e4 = P.add("vector", lambda v: v.tensor_sub(r[3], r[1], r[0]), mark="V")
            P.add("scalar", lambda s: s.activation(out=r[0], in_=r[3], func=AF.Ln,
                                                   bias=g_cst[0:1, 0:1]),
                  waits=[e4])
            e6 = P.add("scalar", lambda s: s.activation(out=r[4], in_=r[0],
                                                        func=AF.Exp,
                                                        scale=g_cst[0:1, 1:2]),
                       mark="S")
            P.add("vector", lambda v: v.tensor_mul(r[5], r[2], r[4]), waits=[e6])
            e8 = P.add("vector", lambda v: v.tensor_scalar_mul(r[5], r[5], -1.0),
                       mark="V")

            def repf(t):
                t.matmul(pb[6][:, :], g_onesrow[:], r[4], start=True, stop=True)
                return t.matmul(pb[7][:, :], g_onesrow[:], r[5], start=True,
                                stop=True)
            mv2 = P.add("tensor", repf, waits=[e8], mark="T")
            ea = P.add("scalar", lambda s: s.copy(out=lnrep[:, 0, :],
                                                  in_=pb[6][:, :]),
                       waits=[mv2], mark="S")
            eb = P.add("scalar", lambda s: s.copy(out=lnrep[:, 1, :],
                                                  in_=pb[7][:, :]), mark="S")
            banklast[6] = eb
            banklast[7] = eb
            outs = []
            for i in range(nd):
                P.add("vector", lambda v, i=i: v.tensor_mul(
                    lnnrm[:, i, :], xap(i), lnrep[:, 0, :]), waits=[ea, eb])
                ev = P.add("vector", lambda v, i=i: v.tensor_add(
                    lnnrm[:, i, :], lnnrm[:, i, :], lnrep[:, 1, :]), mark="V")
                outs += sink(i, lnnrm[:, i, :], [ev])
            return outs
        return emit_ln, ln_free

    def tapdump(name, get_src, ntile, waits):
        if name not in taps:
            return
        for i in range(max(ntile, 1)):
            if ntile > 1:
                d = t128(taps[name])[:, i, :]
            else:
                d = taps[name][:]
            P.add("gpsimd", lambda s, i=i, d=d: s.dma_start(out=d,
                                                            in_=get_src(i)),
                  waits=waits, mark="dG")

    import os as _os
    if _os.environ.get("KN_FAKE1"):
        P.add("gpsimd", lambda g: g.dma_start(out=cc_in, in_=xT),
              waits=P.bar(), mark="dG")
        agev = P.add("gpsimd", lambda g: g.collective_compute(
            "AllGather", ALU.bypass, ins=[cc_in], outs=[cc_out],
            replica_groups=[[0, 1, 2, 3, 4, 5, 6, 7]]), waits=P.bar(), mark="G")
        globals()["_FAKE_AG"] = True
    emit_ln1, lnfree1 = make_ln("p1", True)
    xch = [AR.al(f"xch{i}", [128, 8, CW], F32) for i in range(2)]
    hnc = AR.al("hnc", [128, 8, CW], F16)
    hnRc = AR.al("hnRc", [128, 8, CW], F16)

    prev_sink = P.bar()
    for c in range(CH if not _os.environ.get("KN_FAKE1") else 0):
        lx = P.add("sync", lambda s, c=c: s.dma_start(
            out=xch[c % 2][:], in_=t128(xT)[:, :, c * CW:(c + 1) * CW]),
            waits=prev_sink, mark="dS")

        def sink1(i, t16, deps, c=c):
            e1 = P.add("vector", lambda v, i=i, t16=t16: v.tensor_scalar(
                out=hnc[:, i, :], in0=t16, scalar1=g_lng[:, i:i + 1],
                scalar2=g_lnb[:, i:i + 1], op0=ALU.mult, op1=ALU.add),
                waits=deps, mark="V")
            e2 = P.add("vector", lambda v, i=i: v.tensor_copy(
                out=hnRc[:, i, :], in_=hnc[:, i, :][:, ::-1]), waits=[e1], mark="V")
            e3 = P.add("sync", lambda s, i=i, c=c: s.dma_start(
                out=hnsp["f"][i, :, c * CW:(c + 1) * CW], in_=hnc[:, i, :]),
                waits=[e1], mark="dS")
            e4 = P.add("sync", lambda s, i=i, c=c: s.dma_start(
                out=hnsp["b"][i, :, (CH - 1 - c) * CW:(CH - c) * CW],
                in_=hnRc[:, i, :]), waits=[e2], mark="dS")
            return [e3, e4]

        prev_sink = emit_ln1(8, lambda i, c=c: xch[c % 2][:, i, :], False, sink1,
                             pre=[lx])
    tapdump("hn", lambda i: hnsp["f"][i], 8, P.bar())
    lnfree1()
    AR.fr("xch0", "xch1", "hnc", "hnRc")
    if not want("branchf"):
        return _finish(nc, ctx, P, (sems, taps))

    def branch(p):
        ent = P.bar()
        xrb = AR.al("xrb", [128, 16, 2054], F16)
        inwh = AR.al("inwh", [128, 8, DI], F16)
        hnst = [AR.al(f"hnst{i}", [128, 8, CW], F16) for i in range(2)]
        evh = [AR.al(f"evh{i}", [128, CW], F16) for i in range(2)]

        xr_zero = P.add("vector", lambda v: v.memset(xrb[:], 0.0), waits=ent,
                        mark="V")
        cwld = P.add("gpsimd", lambda g: g.dma_start(
            out=g_conv[:], in_=br_in[p]["convd"]), waits=ent, mark="dG")
        xwld = P.add("gpsimd", lambda g: g.dma_start(
            out=g_xproj[:], in_=br_in[p]["xprojT"]), waits=ent, mark="dG")
        dwld = P.add("gpsimd", lambda g: g.dma_start(
            out=g_dtw[:], in_=br_in[p]["dtwT"]), waits=ent, mark="dG")

        hs_t = hnsp[p].transpose([1, 0, 2])
        evev = {}
        szslot = {}
        for half in range(2):
            wev = P.add("gpsimd", lambda g, half=half: g.dma_start(
                out=inwh[:],
                in_=t128(br_in[p]["inwT"])[:, :, half * DI:(half + 1) * DI]),
                waits=ent + ([evev[(0, 63)]] if half else []), mark="dG")
            hld = [None] * CH
            for c in range(2):
                hld[c] = P.add("sync", lambda s, c=c: s.dma_start(
                    out=hnst[c % 2][:], in_=hs_t[:, :, c * CW:(c + 1) * CW]),
                    waits=[wev] + ent, mark="dS")
            for c in range(CH):
                if c >= 2:
                    hld[c] = P.add("sync", lambda s, c=c: s.dma_start(
                        out=hnst[c % 2][:], in_=hs_t[:, :, c * CW:(c + 1) * CW]),
                        waits=[evev[(half, (c - 1) * 16 + 15)]], mark="dS")
                for fo in range(16):
                    g = (half, c * 16 + fo)
                    gi = c * 16 + fo

                    def mmfn(t, c=c, fo=fo, gi=gi):
                        for kt in range(8):
                            v = t.matmul(pb[gi % 4][:],
                                         inwh[:, kt, fo * 128:(fo + 1) * 128],
                                         hnst[c % 2][:, kt, :],
                                         start=(kt == 0), stop=(kt == 7))
                        return v
                    if half == 0:
                        evev[g] = mmg(gi % 4, mmfn, [wev, hld[c], xr_zero],
                                      "scalar",
                                      lambda s, c=c, fo=fo, gi=gi: s.copy(
                                          out=xrb[:, fo,
                                                  3 + c * CW:3 + (c + 1) * CW],
                                          in_=pb[gi % 4][:]))
                    else:
                        ew = [szslot[gi % 2]] if gi % 2 in szslot else []
                        evev[g] = mmg(gi % 4, mmfn, [wev, hld[c]], "scalar",
                                      lambda s, gi=gi: s.activation(
                                          out=evh[gi % 2][:], in_=pb[gi % 4][:],
                                          func=AF.Silu), evac_waits=ew)
                        szslot[gi % 2] = P.add(
                            "gpsimd", lambda gp, c=c, fo=fo, gi=gi: gp.dma_start(
                                out=szsp[fo, :, c * CW:(c + 1) * CW],
                                in_=evh[gi % 2][:]),
                            waits=[evev[g]], mark="dG")
        if p == "f":
            tapdump("xr", lambda i: xrb[:, i, 3:3 + L], 16, P.bar())

        AR.fr("inwh", "hnst0", "hnst1", "evh0", "evh1")
        ub = AR.al("ub", [128, 16, L], F16)
        uev = {}
        for dt_i in range(16):
            for c in range(CH):
                g = dt_i * CH + c

                def cfn(t, dt_i=dt_i, c=c, g=g):
                    for tap in range(4):
                        v = t.matmul(pb[g % 4][:], g_conv[:, dt_i, tap, :],
                                     xrb[:, dt_i, c * CW + tap:c * CW + tap + CW],
                                     start=(tap == 0), stop=(tap == 3))
                    return v
                w3 = [evev[(0, c * 16 + dt_i)], cwld]
                if c > 0:
                    w3.append(evev[(0, (c - 1) * 16 + dt_i)])
                uev[(dt_i, c)] = mmg(g % 4, cfn, w3, "scalar",
                                     lambda s, dt_i=dt_i, c=c, g=g: s.activation(
                                         out=ub[:, dt_i, c * CW:(c + 1) * CW],
                                         in_=pb[g % 4][:], func=AF.Silu,
                                         bias=g_convb[p][:, dt_i:dt_i + 1]))
        if p == "f":
            tapdump("u", lambda i: ub[:, i, :], 16, P.bar())

        AR.fr("xrb")
        dbc16 = AR.al("dbc16", [96, L], F16)
        dbev = []
        for c in range(CH):
            w4 = [uev[(kt, c)] for kt in range(16)] + [xwld]

            def xpf(t, c=c):
                for kt in range(16):
                    v = t.matmul(pb[4 + c % 2][:96, :], g_xproj[:, kt, :],
                                 ub[:, kt, c * CW:(c + 1) * CW],
                                 start=(kt == 0), stop=(kt == 15))
                return v
            dbev.append(mmg(4 + c % 2, xpf, w4, "scalar",
                            lambda s, c=c: s.copy(
                                out=dbc16[:, c * CW:(c + 1) * CW],
                                in_=pb[4 + c % 2][:96, :])))
        if p == "f":
            tapdump("dbc", lambda i: dbc16[:, :], 1, P.bar())

        dtpg = [AR.al(f"dtpg{i}", [128, L], F16) for i in range(2)]
        e1b = [AR.al(f"e1b{i}", [128, CW], F32) for i in range(2)]
        dtuc = AR.al("dtuc", [128, L], F16)
        spl = {}
        duev = {}
        for dt_i in range(16):
            ch_ev = []
            for c in range(CH):
                g = dt_i * CH + c
                w6 = []
                if dt_i >= 2 and c == 0:
                    w6.append(spl[dt_i - 2])

                def dtev(s, dt_i=dt_i, c=c, g=g):
                    s.activation(out=e1b[g % 2][:], in_=pb[g % 4][:], func=AF.Exp,
                                 bias=g_dtb[p][:, dt_i:dt_i + 1])
                    return s.activation(
                        out=dtpg[dt_i % 2][:, c * CW:(c + 1) * CW],
                        in_=e1b[g % 2][:], func=AF.Ln, bias=1.0)
                ch_ev.append(mmg(g % 4, lambda t, dt_i=dt_i, c=c, g=g: t.matmul(
                    pb[g % 4][:], g_dtw[:, dt_i * 128:(dt_i + 1) * 128],
                    dbc16[:DTR, c * CW:(c + 1) * CW], start=True, stop=True),
                    list(dbev) + [dwld], "scalar", dtev, evac_waits=w6))
            du = P.add("vector", lambda v, dt_i=dt_i: v.tensor_mul(
                dtuc[:], dtpg[dt_i % 2][:], ub[:, dt_i, :]),
                waits=ch_ev + ([duev[dt_i - 1]] if dt_i >= 1 else []), mark="V")
            duev[dt_i] = du
            P.add("sync", lambda s, dt_i=dt_i: s.dma_start(
                out=dtsp[dt_i], in_=dtpg[dt_i % 2][:]), waits=[du], mark="dS")
            P.add("sync", lambda s, dt_i=dt_i: s.dma_start(
                out=usp[dt_i], in_=ub[:, dt_i, :]), waits=[du], mark="dS")
            spl[dt_i] = P.add("sync", lambda s, dt_i=dt_i: s.dma_start(
                out=dusp[dt_i], in_=dtuc[:]), waits=[du], mark="dS")

        AR.fr("ub", "dtpg0", "dtpg1", "e1b0", "e1b1", "dtuc")
        yac = AR.al("yac", [128, 16, L], F16)
        brp = AR.al("brp", [128, 2, 2, L], F16)
        crp = AR.al("crp", [128, 2, 2, L], F16)
        dtt = [AR.al(f"dtt{i}", [128, L], F16) for i in range(2)]
        dut = [AR.al(f"dut{i}", [128, L], F16) for i in range(2)]
        ab = [AR.al(f"ab{i}", [128, L], F16) for i in range(2)]
        bb_ = [AR.al(f"bb{i}", [128, L], F16) for i in range(2)]
        hb = [AR.al(f"hb{i}", [128, L], F16) for i in range(2)]
        tb = [AR.al(f"tb{i}", [128, L], F16) for i in range(2)]
        fence_e = P.bar()

        yinit = {}
        for dt_i in range(16):
            lu = P.add("sync", lambda s, dt_i=dt_i: s.dma_start(
                out=dtt[dt_i % 2][:], in_=usp[dt_i]),
                waits=fence_e + ([yinit[dt_i - 2]] if dt_i >= 2 else []),
                mark="dS")
            yinit[dt_i] = P.add("vector", lambda v, dt_i=dt_i: v.tensor_scalar_mul(
                yac[:, dt_i, :], dtt[dt_i % 2][:], g_dvec[p][:, dt_i:dt_i + 1]),
                waits=[lu], mark="V")
        yi_all = list(yinit.values())

        NGW = 2
        NG = 16 // NGW
        repev = {}
        scan_ev = {}
        yadd_ev = {}
        bmul_ev = {}
        ymul_ev = {}
        aev = {}
        for ng in range(NG):
            for i in range(NGW):
                n = ng * NGW + i
                for bc in range(2):
                    dest = brp if bc == 0 else crp
                    sel = n if bc == 0 else 16 + n
                    last = None
                    for c in range(CH):
                        gg = 4 + (((n * 2 + bc) * CH + c) % 2)
                        w7 = list(dbev) + fence_e
                        if ng >= 2:
                            w7.append(ymul_ev.get((ng - 2, 15, NGW - 1)))
                            w7.append(bmul_ev.get((ng - 2, 15, NGW - 1)))
                        last = mmg(gg, lambda t, sel=sel, c=c, gg=gg: t.matmul(
                            pb[gg][:], g_selBC[:, sel, :],
                            dbc16[:, c * CW:(c + 1) * CW], start=True, stop=True),
                            w7, "scalar",
                            lambda s, dest=dest, ng=ng, i=i, c=c, gg=gg: s.copy(
                                out=dest[:, ng % 2, i, c * CW:(c + 1) * CW],
                                in_=pb[gg][:]))
                    repev[(ng, i, bc)] = last

            for dt_i in range(16):
                it0 = ng * 16 + dt_i
                wl = (list(fence_e) + yi_all) if it0 < 2 else \
                    [aev[((it0 - 2) // 16, (it0 - 2) % 16, NGW - 1)]]
                dl = P.add("sync", lambda s, dt_i=dt_i: s.dma_start(
                    out=dtt[dt_i % 2][:], in_=dtsp[dt_i]), waits=wl, mark="dS")
                wl2 = (list(fence_e) + yi_all) if it0 < 2 else \
                    [bmul_ev[((it0 - 2) // 16, (it0 - 2) % 16, NGW - 1)]]
                dul = P.add("sync", lambda s, dt_i=dt_i: s.dma_start(
                    out=dut[dt_i % 2][:], in_=dusp[dt_i]), waits=wl2, mark="dS")
                for i in range(NGW):
                    n = ng * NGW + i
                    it = it0 * NGW + i
                    wa = [dl] + negA
                    if it >= 2:
                        wa.append(scan_ev[it - 2])
                    aev[(ng, dt_i, i)] = P.add(
                        "scalar", lambda s, dt_i=dt_i, n=n, it=it: s.activation(
                            out=ab[it % 2][:], in_=dtt[dt_i % 2][:], func=AF.Exp,
                            scale=g_A[p][:, dt_i, n:n + 1]), waits=wa, mark="S")
                    wb = [repev[(ng, i, 0)], dul]
                    if it >= 2:
                        wb.append(scan_ev[it - 2])
                    bmul_ev[(ng, dt_i, i)] = P.add(
                        "gpsimd", lambda g, dt_i=dt_i, ng=ng, i=i, it=it:
                        g.tensor_mul(bb_[it % 2][:], dut[dt_i % 2][:],
                                     brp[:, ng % 2, i, :]), waits=wb, mark="G")
                    ws = [aev[(ng, dt_i, i)], bmul_ev[(ng, dt_i, i)]]
                    if it >= 2:
                        ws.append(ymul_ev[it - 2])
                    scan_ev[it] = P.add(
                        "vector", lambda v, it=it: v.tensor_tensor_scan(
                            out=hb[it % 2][:], data0=ab[it % 2][:],
                            data1=bb_[it % 2][:], initial=0.0,
                            op0=ALU.mult, op1=ALU.add), waits=ws, mark="V")
                    wy = [scan_ev[it], repev[(ng, i, 1)]]
                    if it >= 2:
                        wy.append(yadd_ev[it - 2])
                    ymul_ev[it] = ymul_ev[(ng, dt_i, i)] = P.add(
                        "gpsimd", lambda g, ng=ng, i=i, it=it: g.tensor_mul(
                            tb[it % 2][:], hb[it % 2][:], crp[:, ng % 2, i, :]),
                        waits=wy, mark="G")
                    yadd_ev[it] = P.add(
                        "vector", lambda v, dt_i=dt_i, it=it: v.tensor_add(
                            yac[:, dt_i, :], yac[:, dt_i, :], tb[it % 2][:]),
                        waits=[ymul_ev[it], yinit[dt_i]], mark="V")
        if p == "f":
            tapdump("yact", lambda i: yac[:, i, :], 16, P.bar())

        AR.fr("brp", "crp", "dtt0", "dtt1", "dut0", "dut1", "ab0", "ab1",
              "bb0", "bb1", "hb0", "hb1", "tb0", "tb1", "dbc16")
        ow16 = AR.al("ow16", [128, 16, DM], F16)
        szt = [AR.al(f"szt{i}", [128, L], F16) for i in range(2)]
        oev = [AR.al(f"oev{i}", [128, CW], F16) for i in range(2)]
        orv = AR.al("orv", [128, CW], F16)
        fence_f = P.bar()

        gev = {}
        owev = P.add("gpsimd", lambda s: s.dma_start(
            out=ow16, in_=t128(br_in[p]["outwT"])), waits=fence_f, mark="dG")
        for dt_i in range(16):
            ydone = yadd_ev[((NG - 1) * 16 + dt_i) * NGW + NGW - 1]
            szl = P.add("sync", lambda s, dt_i=dt_i: s.dma_start(
                out=szt[dt_i % 2], in_=szsp[dt_i]),
                waits=[ydone] + fence_f +
                ([gev[dt_i - 2]] if dt_i >= 2 else []), mark="dS")
            gev[dt_i] = P.add("vector", lambda v, dt_i=dt_i: v.tensor_mul(
                yac[:, dt_i, :], szt[dt_i % 2], yac[:, dt_i, :]),
                waits=[szl, ydone], mark="V")
        if p == "f":
            tapdump("g", lambda i: yac[:, i, :], 16, P.bar())

        oslot = {}
        for dm in range(8):
            for c in range(CH):
                g = dm * CH + c

                def opf(t, dm=dm, c=c, g=g):
                    for kt in range(16):
                        v = t.matmul(pb[g % 4][:],
                                     ow16[:, kt, dm * 128:(dm + 1) * 128],
                                     yac[:, kt, c * CW:(c + 1) * CW],
                                     start=(kt == 0), stop=(kt == 15))
                    return v
                se = mmg(g % 4, opf, [owev] + [gev[kt] for kt in range(16)],
                         "scalar",
                         lambda s, g=g: s.copy(out=oev[g % 2],
                                               in_=pb[g % 4][:]),
                         evac_waits=[oslot.get(g % 2)])
                if p == "f":
                    oslot[g % 2] = P.add(
                        "sync", lambda s, dm=dm, c=c, g=g: s.dma_start(
                            out=osp["f"][dm, :, c * CW:(c + 1) * CW],
                            in_=oev[g % 2]), waits=[se], mark="dS")
                else:
                    rv = P.add("vector", lambda v, g=g: v.tensor_copy(
                        out=orv, in_=oev[g % 2][:, ::-1]), waits=[se],
                        mark="V")
                    oslot[g % 2] = rv
                    P.add("sync", lambda s, dm=dm, c=c: s.dma_start(
                        out=osp["b"][dm, :, (CH - 1 - c) * CW:(CH - c) * CW],
                        in_=orv), waits=[rv], mark="dS")
        AR.fr("yac", "ow16", "szt0", "szt1", "oev0", "oev1", "orv")

    if _os.environ.get("KN_FAKE1"):
        lnfree1()
        AR.fr("xch0", "xch1", "hnc", "hnRc")
        return _build_phase2(**locals()) if False else None or _phase2_entry()
    branch("f")
    if not want("branchb"):
        tapdump("o_f", lambda i: osp["f"][i], 8, P.bar())
        return _finish(nc, ctx, P, (sems, taps))
    branch("b")
    tapdump("o_f", lambda i: osp["f"][i], 8, P.bar())
    tapdump("o_b", lambda i: osp["b"][i], 8, P.bar())
    if not want("p4"):
        return _finish(nc, ctx, P, (sems, taps))

    emit_ln4, lnfree4 = make_ln("p4", False)
    ofb = AR.al("ofb", [128, 8, L], F16)
    obb = AR.al("obb", [128, 8, L], F16)
    proj16 = AR.al("proj16", [128, 16, DM], F16)
    xc4 = AR.al("xc4", [128, 8, CW], F16)
    r4 = AR.al("r4", [128, 8, CW], F16)
    x1c16 = AR.al("x1c16", [128, 8, CW], F16)
    e1c = [AR.al(f"e1c{i}", [128, CW], F32) for i in range(2)]
    fence4 = P.bar()

    pjev = P.add("gpsimd", lambda g: g.dma_start(out=proj16[:], in_=t128(projT)),
                 waits=fence4, mark="dG")
    old = []
    for dm in range(8):
        old.append(P.add("sync", lambda s, dm=dm: s.dma_start(
            out=ofb[:, dm, :], in_=osp["f"][dm]), waits=fence4, mark="dS"))
        old.append(P.add("sync", lambda s, dm=dm: s.dma_start(
            out=obb[:, dm, :], in_=osp["b"][dm]), waits=fence4, mark="dS"))

    prev4 = []
    for c in range(CH):
        lx = P.add("gpsimd", lambda s, c=c: s.dma_start(
            out=xc4, in_=t128(xT)[:, :, c * CW:(c + 1) * CW]),
            waits=fence4 + prev4, mark="dG")
        pr_ev = {}
        for dm in range(8):
            def pjf(t, dm=dm, c=c):
                for kt in range(16):
                    rhs = (ofb[:, kt, c * CW:(c + 1) * CW] if kt < 8 else
                           obb[:, kt - 8, c * CW:(c + 1) * CW])
                    v = t.matmul(pb[dm % 4][:],
                                 proj16[:, kt, dm * 128:(dm + 1) * 128],
                                 rhs, start=(kt == 0), stop=(kt == 15))
                return v

            def pjev_(v, dm=dm):
                return v.tensor_scalar_add(e1c[dm % 2][:], pb[dm % 4][:],
                                           g_projb[:, dm:dm + 1])
            se = mmg(dm % 4, pjf, [pjev, lx] + old, "vector", pjev_,
                     evac_waits=[pr_ev.get(dm - 2)])
            pr_ev[dm] = P.add("vector",
                              lambda v, dm=dm, c=c: v.scalar_tensor_tensor(
                                  out=r4[:, dm, :], in0=xc4[:, dm, :], scalar=2.0,
                                  in1=e1c[dm % 2][:], op0=ALU.mult, op1=ALU.add),
                              waits=[se], mark="V")

        def sink4(i, t16, deps, c=c):
            e1 = P.add("scalar", lambda s, i=i, t16=t16: s.copy(
                out=x1c16[:, i, :], in_=t16), waits=deps, mark="S")
            e2 = P.add("sync", lambda s, i=i, c=c: s.dma_start(
                out=t128(cc_in)[:, i, c * CW:(c + 1) * CW], in_=x1c16[:, i, :]),
                waits=[e1], mark="dS")
            return [e2]

        prev4 = emit_ln4(8, lambda i: r4[:, i, :], True, sink4,
                         pre=[pr_ev[i] for i in range(8)])
    tapdump("x1", lambda i: t128(cc_in)[:, i, :], 8, P.bar())
    lnfree4()
    if not want("ag"):
        return _finish(nc, ctx, P, (sems, taps))
    AR.fr("ofb", "obb", "proj16", "xc4", "r4", "x1c16", "e1c0", "e1c1")

    agev = P.add("gpsimd", lambda g: g.collective_compute(
        "AllGather", ALU.bypass, ins=[cc_in], outs=[cc_out],
        replica_groups=[[0, 1, 2, 3, 4, 5, 6, 7]]), waits=P.bar(), mark="G")
    if not want("qkv"):
        return _finish(nc, ctx, P, (sems, taps))

    _sl = []

    def ldy0_s(dst, s_i, waits):
        def f(s, dst=dst, s_i=s_i):
            if not _sl:
                pid = nc.partition_id((mybir.EngineType.SP,))
                _sl.append(bass.ts(pid, SL))
            a = cc_out[s_i, :, _sl[0]].rearrange("(f p) q -> p f q", p=128)
            return s.dma_start(out=dst, in_=a)
        return P.add("sync", f, waits=waits, mark="dS")

    y0a = AR.al("y0a", [128, 8, TOK], F16)
    qb = AR.al("qb", [128, 8, TOK], F16)
    kb = AR.al("kb", [128, 8, TOK], F16)
    vb = AR.al("vb", [128, 8, TOK], F16)
    ipw3 = AR.al("ipw3", [128, 8, DM], F16)

    ldy = [ldy0_s(y0a[:, :, s_i * SL:(s_i + 1) * SL], s_i, [agev])
           for s_i in range(8)]
    for ft in range(8):
        P.add("sync", lambda s, ft=ft: s.dma_start(
            out=y0sp[ft], in_=y0a[:, ft, :]), waits=ldy, mark="dS")
    qkev = {}
    ipev = {}
    for third, dest in ((0, qb), (1, kb), (2, vb)):
        ipev[third] = P.add("gpsimd", lambda g, third=third: g.dma_start(
            out=ipw3[:], in_=t128(inprojT)[:, :, third * DM:(third + 1) * DM]),
            waits=[agev] + ([qkev[(third - 1, 31)]] if third else []), mark="dG")
        for fo in range(8):
            for c in range(CH):
                g = fo * CH + c

                def qf(t, fo=fo, c=c, g=g):
                    for kt in range(8):
                        v = t.matmul(pb[g % 4][:],
                                     ipw3[:, kt, fo * 128:(fo + 1) * 128],
                                     y0a[:, kt, c * CW:(c + 1) * CW],
                                     start=(kt == 0), stop=(kt == 7))
                    return v
                qkev[(third, g)] = mmg(
                    g % 4, qf, [ipev[third]] + ldy, "vector",
                    lambda v, third=third, dest=dest, fo=fo, c=c, g=g:
                    v.tensor_scalar_add(dest[:, fo, c * CW:(c + 1) * CW],
                                        pb[g % 4][:],
                                        g_inprojb[:, third * 8 + fo:
                                                  third * 8 + fo + 1]))
    if "qkv" in taps:
        tapdump("qkv", lambda i: (qb, kb, vb)[i // 8][:, i % 8, :], 24, P.bar())
    if not want("attn"):
        return _finish(nc, ctx, P, (sems, taps))

    AR.fr("y0a", "ipw3")
    mbb = AR.al("mbb", [128, 2, 8, SL], F16)
    Ss = AR.al("Ss", [16, 8 * SL], F32)
    S2s = AR.al("S2s", [16, 8 * SL], F32)
    Mrw = AR.al("Mrw", [16, SL], F32)
    ATs = AR.al("ATs", [16, 8 * SL], F16)
    ctxb = AR.al("ctxb", [128, 8, TOK], F16)
    ctmp = [AR.al(f"ctmp{i}", [128, SL], F16) for i in range(2)]

    import os as _os2
    SLIM = int(_os2.environ.get("KN_SLIM", "8"))
    NOSM = _os2.environ.get("KN_NOSM")
    NOCTX = _os2.environ.get("KN_NOCTX")
    TLIM = int(_os2.environ.get("KN_TLIM", "8"))
    NOMM = _os2.environ.get("KN_NOMM")
    cxev = {}
    sft_prev = None
    for s_i in range(SLIM):
        sev = {}
        for t_i in range(TLIM):
            pi = s_i * 8 + t_i
            mev = None
            for ft in range(8):
                w11 = [qkev[(0, ft * CH + s_i // 2)],
                       qkev[(1, ft * CH + t_i // 2)]]
                if pi >= 2:
                    w11.append(sev.get(t_i - 2) if t_i >= 2 else
                               cxev.get((s_i - 1, 7)))
                mev = P.add("vector", lambda v, s_i=s_i, t_i=t_i, ft=ft, pi=pi:
                            v.tensor_mul(
                                mbb[:, pi % 2, ft, :],
                                qb[:, ft, s_i * SL:(s_i + 1) * SL],
                                kb[:, ft, t_i * SL:(t_i + 1) * SL]),
                            waits=w11, mark="V" if ft == 7 else None)

            def smf(t, pi=pi):
                _sb2 = 4
                for kt in range(8):
                    v = t.matmul(pb[_sb2][:16, :SL], g_headsel[:, kt, :],
                                 mbb[:, pi % 2, kt, :],
                                 start=(kt == 0), stop=(kt == 7))
                return v
            if NOMM:
                sev[t_i] = mev
                continue
            _sb = 4
            _ee = "vector" if _os.environ.get("KN_VEVAC") else "scalar"
            def _sevac(e, t_i=t_i, _sb=_sb):
                if _os.environ.get("KN_VEVAC"):
                    return e.tensor_copy(out=Ss[:, t_i * SL:(t_i + 1) * SL],
                                         in_=pb[_sb][:16, :SL])
                return e.copy(out=Ss[:, t_i * SL:(t_i + 1) * SL],
                              in_=pb[_sb][:16, :SL])
            sev[t_i] = mmg(_sb, smf, [mev], _ee, _sevac,
                           evac_waits=[sft_prev])
        if s_i == 0 and not NOMM and TLIM == 8:
            tapdump("S", lambda i: Ss, 1, [sev[7]])
        if NOSM:
            sft_prev = sev[TLIM - 1]
            continue
        em = P.add("vector", lambda v: v.tensor_reduce(
            out=Mrw.rearrange("h (q o) -> h q o", o=1),
            in_=Ss.rearrange("h (t q) -> h t q", t=8).transpose([0, 2, 1]),
            axis=mybir.AxisListType.X, op=ALU.max),
            waits=[sev[t_] for t_ in range(8)], mark="V")
        es = P.add("vector", lambda v: v.tensor_sub(
            S2s.rearrange("h (t q) -> h t q", t=8),
            Ss.rearrange("h (t q) -> h t q", t=8),
            Mrw.rearrange("h (o q) -> h o q", o=1).broadcast_to((16, 8, SL))),
            waits=[em], mark="V")
        ee = P.add("scalar", lambda s: s.activation(out=S2s, in_=S2s,
                                                    func=AF.Exp),
                   waits=[es], mark="S")
        e_sum = P.add("vector", lambda v: v.tensor_reduce(
            out=Mrw.rearrange("h (q o) -> h q o", o=1),
            in_=S2s.rearrange("h (t q) -> h t q", t=8).transpose([0, 2, 1]),
            axis=mybir.AxisListType.X, op=ALU.add), waits=[ee], mark="V")
        e_rcp = P.add("vector", lambda v: v.reciprocal(out=Mrw, in_=Mrw),
                      waits=[e_sum], mark="V")
        sft_prev = P.add("vector", lambda v: v.tensor_mul(
            ATs.rearrange("h (t q) -> h t q", t=8),
            S2s.rearrange("h (t q) -> h t q", t=8),
            Mrw.rearrange("h (o q) -> h o q", o=1).broadcast_to((16, 8, SL))),
            waits=[e_rcp], mark="V")

        if NOCTX:
            continue
        for ft in range(8):
            for t_i in range(8):
                g = (s_i * 8 + ft) * 8 + t_i

                def xpf2(t, t_i=t_i, ft=ft, g=g):
                    return t.matmul(pb[4 + g % 2][:, :SL], g_headexp[:, ft, :],
                                    ATs[:, t_i * SL:(t_i + 1) * SL],
                                    start=True, stop=True)

                def cev(v, s_i=s_i, t_i=t_i, ft=ft, g=g):
                    v.tensor_mul(ctmp[g % 2], pb[4 + g % 2][:, :SL],
                                 vb[:, ft, t_i * SL:(t_i + 1) * SL])
                    if t_i == 0:
                        return v.tensor_copy(
                            out=ctxb[:, ft, s_i * SL:(s_i + 1) * SL],
                            in_=ctmp[g % 2])
                    return v.tensor_add(
                        ctxb[:, ft, s_i * SL:(s_i + 1) * SL],
                        ctxb[:, ft, s_i * SL:(s_i + 1) * SL], ctmp[g % 2])
                cxev[(s_i, ft)] = mmg(4 + g % 2, xpf2, [sft_prev], "vector", cev)
    ctxdone = ([cxev[(SLIM - 1, ft)] for ft in range(8)]
               if cxev else [sft_prev])
    tapdump("ctx", lambda i: ctxb[:, i, :], 8, P.bar())
    AR.fr("qb", "kb", "vb", "mbb", "Ss", "S2s", "Mrw", "ATs", "ctmp0", "ctmp1")
    if not want("ln1"):
        return _finish(nc, ctx, P, (sems, taps))

    emit_ln2, lnfree2 = make_ln("l1", False)
    ow2 = AR.al("ow2", [128, 8, DM], F16)
    x2b = AR.al("x2b", [128, 8, TOK], F16)
    y0c = AR.al("y0c", [128, 8, CW], F16)
    r2b = AR.al("r2b", [128, 8, CW], F16)
    aot = [AR.al(f"aot{i}", [128, CW], F32) for i in range(2)]
    fence2c = P.bar()

    owev2 = P.add("gpsimd", lambda g: g.dma_start(out=ow2[:], in_=t128(outw2T)),
                  waits=fence2c, mark="dG")
    x2done = []
    prevx2 = []
    for c in range(CH):
        y0l = [P.add("sync", lambda s, ft=ft, c=c: s.dma_start(
            out=y0c[:, ft, :], in_=y0sp[ft, :, c * CW:(c + 1) * CW]),
            waits=fence2c + prevx2, mark="dS") for ft in range(8)]
        pr_ev = {}
        for dm in range(8):
            def aof(t, dm=dm, c=c):
                for kt in range(8):
                    v = t.matmul(pb[dm % 4][:],
                                 ow2[:, kt, dm * 128:(dm + 1) * 128],
                                 ctxb[:, kt, c * CW:(c + 1) * CW],
                                 start=(kt == 0), stop=(kt == 7))
                return v
            se = mmg(dm % 4, aof, [owev2] + ctxdone, "vector",
                     lambda v, dm=dm: v.tensor_scalar_add(
                         aot[dm % 2], pb[dm % 4][:], g_outb2[:, dm:dm + 1]),
                     evac_waits=[pr_ev.get(dm - 2)])
            pr_ev[dm] = P.add("vector", lambda v, dm=dm, c=c: v.tensor_add(
                r2b[:, dm, :], aot[dm % 2], y0c[:, dm, :]),
                waits=[se, y0l[dm]], mark="V")

        def sinkx2(i, t16, deps, c=c):
            return [P.add("vector", lambda v, i=i, t16=t16, c=c: v.tensor_scalar(
                out=x2b[:, i, c * CW:(c + 1) * CW], in0=t16,
                scalar1=g_ln1g[:, i:i + 1], scalar2=g_ln1b[:, i:i + 1],
                op0=ALU.mult, op1=ALU.add),
                waits=deps, mark="V")]

        prevx2 = emit_ln2(8, lambda i: r2b[:, i, :], True, sinkx2,
                          pre=[pr_ev[i] for i in range(8)])
        x2done += prevx2
    tapdump("x2", lambda i: x2b[:, i, :], 8, P.bar())
    lnfree2()
    if not want("gate"):
        return _finish(nc, ctx, P, (sems, taps))
    AR.fr("ctxb", "ow2", "y0c", "r2b", "aot0", "aot1")

    gtw = AR.al("gtw", [128, 8, NE], F16)
    glog = AR.al("glog", [128, 16, NE], F32)
    gprb = AR.al("gprb", [128, 16, NE], F32)
    gm8 = AR.al("gm8", [128, 16, NE], F32)
    gd = AR.al("gd", [128, 16, NE], F32)
    geq = AR.al("geq", [128, 16, NE], F32)
    gmask = AR.al("gmask", [128, 16, NE], F32)
    gwv = AR.al("gwv", [128, 16, NE], F16)
    grm = AR.al("grm", [128, 16], F32)
    gden = AR.al("gden", [128, 16], F32)
    gnta = AR.al("gnta", [128, 16], F32)
    gnt = AR.al("gnt", [128, 16], F32)
    grec = AR.al("grec", [128, 16], F32)
    combT = AR.al("combT", [8, TOK], F16)
    fence2d = P.bar()

    gwev = P.add("gpsimd", lambda g: g.dma_start(out=gtw[:], in_=t128(gateT)),
                 waits=fence2d, mark="dG")
    ge = {}
    for tt in range(16):
        def gf(t, tt=tt):
            for kt in range(8):
                v = t.matmul(pb[6][:, :NE], x2b[:, kt, tt * 128:(tt + 1) * 128],
                             gtw[:, kt, :], start=(kt == 0), stop=(kt == 7))
            return v

        def gevac(v, tt=tt):
            return v.tensor_add(glog[:, tt, :], pb[6][:, :NE], g_gatebr[:])
        ge[1, tt] = mmg(6, gf, [gwev] + x2done + fence2d, "vector", gevac)
    for tt in range(16):
        ge[2, tt] = P.add("vector", lambda v, tt=tt: v.tensor_reduce(
            out=grm[:, tt:tt + 1], in_=glog[:, tt, :], axis=mybir.AxisListType.X,
            op=ALU.max), waits=[ge[1, tt]], mark="V")
    for tt in range(16):
        ge[3, tt] = P.add("vector", lambda v, tt=tt: v.tensor_tensor(
            out=gd[:, tt, :], in0=glog[:, tt, :],
            in1=grm[:, tt:tt + 1].broadcast_to((128, NE)),
            op=ALU.subtract), waits=[ge[2, tt]], mark="V")
    for tt in range(16):
        ge[4, tt] = P.add("scalar", lambda s, tt=tt: s.activation(
            out=gprb[:, tt, :], in_=gd[:, tt, :], func=AF.Exp),
            waits=[ge[3, tt]], mark="S")
    for tt in range(16):
        ge[5, tt] = P.add("vector", lambda v, tt=tt: v.max(
            out=gm8[:, tt, :], in_=gprb[:, tt, :]), waits=[ge[4, tt]], mark="V")
    for tt in range(16):
        ge[6, tt] = P.add("vector", lambda v, tt=tt: v.tensor_add(
            gden[:, tt:tt + 1], gm8[:, tt, 0:1], gm8[:, tt, 1:2]),
            waits=[ge[5, tt]], mark="V")
    for tt in range(16):
        ge[7, tt] = P.add("vector", lambda v, tt=tt: v.tensor_add(
            gnta[:, tt:tt + 1], gm8[:, tt, 1:2], gm8[:, tt, 2:3]),
            waits=[ge[5, tt]], mark="V")
    for tt in range(16):
        ge[8, tt] = P.add("vector", lambda v, tt=tt: v.tensor_scalar(
            out=gnt[:, tt:tt + 1], in0=gnta[:, tt:tt + 1], scalar1=-0.5,
            scalar2=None, op0=ALU.mult), waits=[ge[7, tt]], mark="V")
    for tt in range(16):
        ge[9, tt] = P.add("vector", lambda v, tt=tt: v.tensor_tensor(
            out=gd[:, tt, :], in0=gprb[:, tt, :],
            in1=gnt[:, tt:tt + 1].broadcast_to((128, NE)), op=ALU.add),
            waits=[ge[8, tt], ge[4, tt]], mark="V")
    for tt in range(16):
        ge[10, tt] = P.add("scalar", lambda s, tt=tt: s.activation(
            out=geq[:, tt, :], in_=gd[:, tt, :], func=AF.Sign),
            waits=[ge[9, tt]], mark="S")
    for tt in range(16):
        ge[11, tt] = P.add("vector", lambda v, tt=tt: v.reciprocal(
            out=grec[:, tt:tt + 1], in_=gden[:, tt:tt + 1]),
            waits=[ge[6, tt]], mark="V")
    for tt in range(16):
        ge[12, tt] = P.add("vector", lambda v, tt=tt: v.tensor_scalar(
            out=gmask[:, tt, :], in0=geq[:, tt, :], scalar1=0.5, scalar2=0.5,
            op0=ALU.mult, op1=ALU.add), waits=[ge[10, tt]], mark="V")
    for tt in range(16):
        ge[13, tt] = P.add("vector", lambda v, tt=tt: v.tensor_mul(
            gd[:, tt, :], gprb[:, tt, :], gmask[:, tt, :]),
            waits=[ge[12, tt], ge[10, tt]], mark="V")
    for tt in range(16):
        ge[14, tt] = P.add("vector", lambda v, tt=tt: v.tensor_tensor(
            out=gwv[:, tt, :], in0=gd[:, tt, :],
            in1=grec[:, tt:tt + 1].broadcast_to((128, NE)), op=ALU.mult),
            waits=[ge[13, tt], ge[11, tt]], mark="V")
    combev = []
    for tt in range(16):
        w15 = [ge[14, tt]]
        if banklast[7] is not None:
            w15.append(banklast[7])
        mv2 = P.add("tensor", lambda t, tt=tt: t.transpose(
            pb[7][:].bitcast(F16)[:NE, :128], gwv[:, tt, :], g_ident[:]),
            waits=w15, mark="T")
        ce = P.add("scalar", lambda s, tt=tt: s.copy(
            out=combT[:, tt * 128:(tt + 1) * 128],
            in_=pb[7][:].bitcast(F16)[:NE, :128]),
            waits=[mv2], mark="S")
        banklast[7] = ce
        combev.append(ce)
    tapdump("comb", lambda i: combT, 1, P.bar())
    if not want("moe"):
        return _finish(nc, ctx, P, (sems, taps))

    AR.fr("gtw", "glog", "gprb", "gm8", "gd", "geq", "gmask", "gwv",
          "grm", "gden", "gnta", "gnt", "grec")
    moeb = AR.al("moeb", [128, 8, TOK], F16)
    h1b = AR.al("h1b", [128, 32, 2 * CW], F16)
    w1b = [AR.al(f"w1b{i}", [128, 8, 128], F16) for i in range(3)]
    w2b = [AR.al(f"w2b{i}", [128, 32, 128], F16) for i in range(2)]
    crep2 = AR.al("crep2", [128, TOK], F16)
    hrel = [AR.al(f"hrel{i}", [128, CW], F16) for i in range(2)]
    hrel2 = [AR.al(f"hrel2{i}", [128, CW], F16) for i in range(2)]
    fence2e = P.bar()

    mz = P.add("vector", lambda v: v.memset(moeb, 0.0), waits=fence2e,
               mark="V")
    w1ld = {}
    w2ld = {}
    h1cons = {}
    h2cons = {}
    moe_last = [mz]
    crep_done = None
    for e in range(NE):
        crev = None
        for c in range(CH):
            crev = mmg(6, lambda t, e=e, c=c: t.matmul(
                pb[6][:, :], g_sel8[:, e, :], combT[:, c * CW:(c + 1) * CW],
                start=True, stop=True),
                [combev[-1], crep_done] + fence2e, "scalar",
                lambda s, c=c: s.copy(out=crep2[:, c * CW:(c + 1) * CW],
                                      in_=pb[6][:, :]))
        crep_ready = crev
        hrel_last = {}
        hrel2_last = {}
        h1done = {}
        for half in range(2):
            for ft in range(32):
                li = (e * 2 + half) * 32 + ft
                wld = []
                if li >= 3:
                    wld.append(w1ld.get(li - 3))
                    wld.append(h1cons.get(li - 3))
                w1ld[li] = P.add("gpsimd", lambda g, e=e, ft=ft, li=li:
                                 g.dma_start(out=w1b[li % 3],
                                             in_=w1t[e, ft].transpose([1, 0, 2])),
                                 waits=wld + fence2e, mark="dG")
                for chh in range(2):
                    c = half * 2 + chh
                    g = ft * 2 + chh

                    def h1f(t, c=c, li=li, g=g):
                        for kt in range(8):
                            v = t.matmul(pb[g % 2][:], w1b[li % 3][:, kt, :],
                                         x2b[:, kt, c * CW:(c + 1) * CW],
                                         start=(kt == 0), stop=(kt == 7))
                        return v

                    def h1ev(s, e=e, ft=ft, chh=chh, g=g):
                        return s.activation(
                            out=h1b[:, ft, chh * CW:(chh + 1) * CW],
                            in_=pb[g % 2][:], func=AF.Relu,
                            bias=g_b1[:, e, ft:ft + 1])
                    h1done[(half, ft)] = h1cons[li] = mmg(
                        g % 2, h1f, [w1ld[li], crep_ready] + x2done,
                        "scalar", h1ev)
            for dm in range(8):
                li2 = (e * 2 + half) * 8 + dm
                wld2 = []
                if li2 >= 2:
                    wld2.append(w2ld.get(li2 - 2))
                    wld2.append(h2cons.get(li2 - 2))
                w2ld[li2] = P.add("gpsimd", lambda g, e=e, dm=dm, li2=li2:
                                  g.dma_start(out=w2b[li2 % 2],
                                              in_=w2t[e, dm].transpose([1, 0, 2])),
                                  waits=wld2 + fence2e, mark="dG")
                for chh in range(2):
                    c = half * 2 + chh
                    g = dm * 2 + chh

                    def h2f(t, chh=chh, li2=li2, g=g):
                        for kt in range(32):
                            v = t.matmul(pb[2 + g % 2][:], w2b[li2 % 2][:, kt, :],
                                         h1b[:, kt, chh * CW:(chh + 1) * CW],
                                         start=(kt == 0), stop=(kt == 31))
                        return v

                    def h2ev(v, e=e, dm=dm, g=g):
                        return v.tensor_scalar_add(hrel[g % 2],
                                                   pb[2 + g % 2][:],
                                                   g_b2[:, e, dm:dm + 1])
                    hw = [w2ld[li2]] + [h1done[(half, ftt)] for ftt in range(32)]
                    se = mmg(2 + g % 2, h2f, hw, "vector", h2ev,
                             evac_waits=[hrel_last.get(g % 2), crep_ready])
                    m1 = hrel_last[g % 2] = P.add(
                        "vector", lambda v, c=c, g=g: v.tensor_mul(
                            hrel2[g % 2], hrel[g % 2],
                            crep2[:, c * CW:(c + 1) * CW]),
                        waits=[se, hrel2_last.get(g % 2)], mark="V")
                    moe_last.append(P.add(
                        "vector", lambda v, dm=dm, c=c, g=g: v.tensor_add(
                            moeb[:, dm, c * CW:(c + 1) * CW],
                            moeb[:, dm, c * CW:(c + 1) * CW], hrel2[g % 2]),
                        waits=[m1, mz], mark="V"))
                    hrel2_last[g % 2] = moe_last[-1]
                    h2cons[li2] = se
        crep_done = moe_last[-1]
    tapdump("crep2", lambda i: crep2[:, :], 1, P.bar())
    tapdump("h1b", lambda i: h1b[:, i, :], 32, P.bar())
    tapdump("moe", lambda i: moeb[:, i, :], 8, P.bar())
    if not want("full"):
        return _finish(nc, ctx, P, (sems, taps))
    AR.fr("w1b0", "w1b1", "w1b2", "w2b0", "w2b1", "h1b", "crep2",
          "hrel0", "hrel1", "hrel20", "hrel21", "combT")

    emit_ln5, lnfree5 = make_ln("f5", False)
    rr16 = AR.al("rr16", [128, 8, CW], F16)
    x3c = AR.al("x3c", [128, 8, CW], F16)
    y0d = AR.al("y0d", [128, 8, CW], F16)
    outc = AR.al("outc", [128, 8, CW], F32)
    fence2f = P.bar()

    st_ev = []
    prevo = []
    for c in range(CH):
        y0l = [P.add("sync", lambda s, ft=ft, c=c: s.dma_start(
            out=y0d[:, ft, :], in_=y0sp[ft, :, c * CW:(c + 1) * CW]),
            waits=fence2f + prevo, mark="dS") for ft in range(8)]
        r2ev = []
        for i in range(8):
            r2ev.append(P.add("vector", lambda v, i=i, c=c: v.tensor_add(
                rr16[:, i, :], x2b[:, i, c * CW:(c + 1) * CW],
                moeb[:, i, c * CW:(c + 1) * CW]),
                waits=fence2f + prevo, mark="V"))

        def sink_ln2(i, t16, deps, c=c):
            return [P.add("vector", lambda v, i=i, t16=t16: v.tensor_scalar(
                out=x3c[:, i, :], in0=t16, scalar1=g_ln2g[:, i:i + 1],
                scalar2=g_ln2b[:, i:i + 1], op0=ALU.mult, op1=ALU.add),
                waits=deps, mark="V")]

        x3done = emit_ln5(8, lambda i: rr16[:, i, :], True, sink_ln2, pre=r2ev)
        r3ev = []
        for i in range(8):
            r3ev.append(P.add("vector", lambda v, i=i, c=c: v.tensor_add(
                rr16[:, i, :], x3c[:, i, :], y0d[:, i, :]),
                waits=x3done + [y0l[i]], mark="V"))

        def sink_out(i, t16, deps, c=c):
            e1 = P.add("scalar", lambda s, i=i, t16=t16: s.copy(
                out=outc[:, i, :], in_=t16), waits=deps, mark="S")
            e2 = P.add("sync", lambda s, i=i, c=c: s.dma_start(
                out=t128(out_h)[:, i, c * CW:(c + 1) * CW], in_=outc[:, i, :]),
                waits=[e1], mark="dS")
            return [e2]

        prevo = emit_ln5(8, lambda i: rr16[:, i, :], True, sink_out, pre=r3ev)
        st_ev += prevo

    P.add("sync", lambda s: s.nop(), waits=st_ev)

    return _finish(nc, ctx, P, block_sems_taps=(sems, taps))


def host_layout(ins):
    f = np.float32
    A = {}

    def col8(v):
        return np.ascontiguousarray(v.reshape(8, 128).T.astype(f))

    def col16(v):
        return np.ascontiguousarray(v.reshape(16, 128).T.astype(f))

    A["lng"] = col8(ins["L0_attn_ln_g"]); A["lnb"] = col8(ins["L0_attn_ln_b"])
    for p, pf in (("f", "fwd"), ("b", "bwd")):
        A[f"inwT_{p}"] = np.ascontiguousarray(ins[f"L0_attn_{pf}_in_w"].T)
        cw = ins[f"L0_attn_{pf}_conv_w"].reshape(16, 128, 4)
        cd = np.zeros((128, 16, 4, 128), f)
        r = np.arange(128)
        cd[r, :, :, r] = cw.transpose(1, 0, 2)[r]
        A[f"convd_{p}"] = cd
        A[f"convb_{p}"] = col16(ins[f"L0_attn_{pf}_conv_b"])
        A[f"xprojT_{p}"] = np.ascontiguousarray(
            ins[f"L0_attn_{pf}_xproj_w"].T.reshape(16, 128, 96)
            .transpose(1, 0, 2))
        A[f"dtwT_{p}"] = np.ascontiguousarray(ins[f"L0_attn_{pf}_dt_w"].T)
        A[f"dtb_{p}"] = col16(ins[f"L0_attn_{pf}_dt_b"])
        A[f"alog_{p}"] = np.ascontiguousarray(
            ins[f"L0_attn_{pf}_A_log"].reshape(16, 128, NS).transpose(1, 0, 2))
        A[f"dvec_{p}"] = col16(ins[f"L0_attn_{pf}_D"])
        A[f"outwT_{p}"] = np.ascontiguousarray(ins[f"L0_attn_{pf}_out_w"].T)
    A["projT"] = np.ascontiguousarray(ins["L0_attn_proj_w"].T)
    A["projb"] = col8(ins["L0_attn_proj_b"])
    A["inprojT"] = np.ascontiguousarray(ins["L0_ffn_in_proj_w"].T)
    A["inprojb"] = np.ascontiguousarray(
        ins["L0_ffn_in_proj_b"].reshape(24, 128).T.astype(f))
    A["outw2T"] = np.ascontiguousarray(ins["L0_ffn_out_w"].T)
    A["outb2"] = col8(ins["L0_ffn_out_b"])
    A["gateT"] = np.ascontiguousarray(ins["L0_ffn_gate_w"].T)
    A["gatebr"] = np.ascontiguousarray(
        np.broadcast_to(ins["L0_ffn_gate_b"], (128, NE)).astype(f))
    A["w1t"] = np.ascontiguousarray(
        ins["L0_ffn_w1"].transpose(0, 2, 1).reshape(NE, 8, 128, 32, 128)
        .transpose(0, 3, 1, 2, 4))
    A["b1h"] = np.ascontiguousarray(
        ins["L0_ffn_b1"].reshape(NE, 32, 128).transpose(0, 2, 1).astype(f))
    A["w2t"] = np.ascontiguousarray(
        ins["L0_ffn_w2"].transpose(0, 2, 1).reshape(NE, 32, 128, 8, 128)
        .transpose(0, 3, 1, 2, 4))
    A["b2h"] = np.ascontiguousarray(
        ins["L0_ffn_b2"].reshape(NE, 8, 128).transpose(0, 2, 1).astype(f))
    A["ln1g"] = col8(ins["L0_ffn_ln1_g"]); A["ln1b"] = col8(ins["L0_ffn_ln1_b"])
    A["ln2g"] = col8(ins["L0_ffn_ln2_g"]); A["ln2b"] = col8(ins["L0_ffn_ln2_b"])

    h = np.float16
    A["ident_in"] = np.eye(128, dtype=h)
    A["ones16"] = np.ones((128, 1), h)
    A["onesrow"] = np.ones((1, 128), f)
    hsel = np.zeros((128, 8, HD), h)
    for kt in range(8):
        for rr in range(128):
            hsel[rr, kt, (kt * 128 + rr) // DH] = 0.125
    A["headsel"] = hsel
    hexp = np.zeros((HD, 8, 128), h)
    for kt in range(8):
        for col in range(128):
            hexp[(kt * 128 + col) // DH, kt, col] = 1.0
    A["headexp"] = hexp
    sbc = np.zeros((96, 32, 128), h)
    for n in range(16):
        sbc[64 + n, n, :] = 1.0
        sbc[80 + n, 16 + n, :] = 1.0
    A["selBC"] = sbc
    s8 = np.zeros((8, NE, 128), h)
    for e in range(NE):
        s8[e, e, :] = 1.0
    A["sel8"] = s8
    A["mrfill"] = np.full((128, NE), 2.0, f)
    A["cst"] = np.ascontiguousarray(
        np.broadcast_to(np.array([1e-5, -0.5], f), (128, 2)))
    return A


_CACHE = {}


def kernel(**inputs):
    ins = {k: np.asarray(v) for k, v in inputs.items()}
    if "nc" not in _CACHE:
        _CACHE["nc"], _CACHE["taps"] = build()
    nc = _CACHE["nc"]
    shared = host_layout(ins)
    x = ins["L0_attn_x"]
    in_maps = []
    for b in range(NB):
        m = dict(shared)
        m["xT"] = np.ascontiguousarray(x[b].T)
        in_maps.append(m)
    res = run_bass_kernel_spmd(nc, in_maps, core_ids=list(range(NB)))
    out = np.empty((NB, L, DM), np.float32)
    for j in range(NB):
        o = np.asarray(res.results[j]["out"])
        out[:, SL * j:SL * (j + 1), :] = o.reshape(DM, 8, SL).transpose(1, 2, 0)
    return out


# version 02
# speedup vs baseline: 1.0045x; 1.0045x over previous
import numpy as np
from contextlib import ExitStack

import concourse.bass as bass
import concourse.mybir as mybir
from concourse.bass_utils import run_bass_kernel_spmd

F32 = mybir.dt.float32
F16 = mybir.dt.float16
AF = mybir.ActivationFunctionType
ALU = mybir.AluOpType

DM, DI, NS, DTR, L = 1024, 2048, 16, 64, 2048
CH, CW = 4, 512
NB, HD, DH, NE, FF = 8, 16, 64, 8, 4096
TOK, SL = 2048, 256

DEBUG_TAPS = ()

ENGINES = ("sync", "tensor", "vector", "scalar", "gpsimd")
SEMS = ("dS", "dG", "T", "V", "S", "G")
DMA_SEMS = ("dS", "dG")


class Ev:
    __slots__ = ("sem", "val")

    def __init__(self, sem, val):
        self.sem = sem
        self.val = val


class Prog:
    def __init__(self):
        self.ops = {e: [] for e in ENGINES}
        self.n = {s: 0 for s in SEMS}

    def add(self, eng, fn, waits=(), mark=None):
        ev = None
        if mark is not None:
            self.n[mark] += 16 if mark in DMA_SEMS else 1
            ev = Ev(mark, self.n[mark])
        self.ops[eng].append((fn, tuple(w for w in waits if w is not None), ev))
        return ev

    def bar(self):
        return [Ev(s, self.n[s]) for s in SEMS if self.n[s] > 0]

    def emit(self, block, sems):
        def run(name):
            def body(eng):
                hw = {s: 0 for s in SEMS}
                for fn, waits, ev in self.ops[name]:
                    for w in waits:
                        if w.val > hw[w.sem]:
                            eng.wait_ge(sems[w.sem], w.val)
                            hw[w.sem] = w.val
                    instr = fn(eng)
                    if ev is not None:
                        instr.then_inc(sems[ev.sem],
                                       16 if ev.sem in DMA_SEMS else 1)
            return body
        block.sync(run("sync"))
        block.tensor(run("tensor"))
        block.vector(run("vector"))
        block.scalar(run("scalar"))
        block.gpsimd(run("gpsimd"))


class Arena:

    _seq = [0]

    def __init__(self, nc, size):
        Arena._seq[0] += 1
        self.t = nc.alloc_sbuf_tensor(f"arena{Arena._seq[0]}", [128, size],
                                      mybir.dt.uint8)
        self.size = size
        self.freel = [(0, size)]
        self.live = {}

    def al(self, name, shape, dtype):
        esz = mybir.dt.size(dtype)
        nb = int(np.prod(shape[1:])) * esz
        nba = (nb + 63) & ~63
        for i, (o, s) in enumerate(self.freel):
            if s >= nba:
                if s == nba:
                    self.freel.pop(i)
                else:
                    self.freel[i] = (o + nba, s - nba)
                ap = self.t.ap()[:shape[0], o:o + nb].bitcast(dtype)
                if len(shape) > 2:
                    dims = {f"d{k}": shape[k] for k in range(1, len(shape))}
                    pat = ("p (" + " ".join(dims) + ") -> p " + " ".join(dims))
                    ap = ap.rearrange(pat, **{k: v for k, v in list(dims.items())[:-1]})
                self.live[name] = (o, nba)
                return ap
        used = self.size - sum(s for _, s in self.freel)
        raise MemoryError(f"arena OOM for {name} ({nba}B; used {used}/{self.size})")

    def fr(self, *names):
        for name in names:
            o, s = self.live.pop(name)
            self.freel.append((o, s))
        self.freel.sort()
        out = []
        for o, s in self.freel:
            if out and out[-1][0] + out[-1][1] == o:
                out[-1] = (out[-1][0], out[-1][1] + s)
            else:
                out.append((o, s))
        self.freel = [tuple(x) for x in out]


def t128(ap):
    return ap.rearrange("(t p) m -> t p m", p=128).transpose([1, 0, 2])




class _Stop(Exception):
    pass


def _finish(nc, ctx, P, block_sems_taps):
    sems, taps = block_sems_taps
    P.add("sync", lambda s: s.nop(), waits=P.bar())
    with nc.Block() as block:
        P.emit(block, sems)
    nc._keepalive = ctx
    return nc, taps


def build(stage="full", debug_taps=()):
    nc = bass.Bass("TRN2", target_bir_lowering=False, debug=False,
                   detect_race_conditions=False)
    nc.cache_partition_id()
    ctx = ExitStack()
    cc = ctx.enter_context
    P = Prog()
    STAGES = ["p1", "branchf", "branchb", "p4", "ag", "qkv", "attn", "ln1",
              "gate", "moe", "full"]
    sidx = STAGES.index(stage)

    def want(st):
        return sidx >= STAGES.index(st)

    def din(name, shape, dt=F32):
        return nc.dram_tensor(name, shape, dt, kind="ExternalInput").ap()

    xT = din("xT", [DM, L])
    lng = din("lng", [128, 8]); lnb = din("lnb", [128, 8])
    br_in = {}
    for p in ("f", "b"):
        br_in[p] = dict(
            inwT=din(f"inwT_{p}", [DM, 2 * DI]),
            convd=din(f"convd_{p}", [128, 16, 4, 128]),
            convb=din(f"convb_{p}", [128, 16]),
            xprojT=din(f"xprojT_{p}", [128, 16, 96]),
            dtwT=din(f"dtwT_{p}", [DTR, DI]),
            dtb=din(f"dtb_{p}", [128, 16]),
            alog=din(f"alog_{p}", [128, 16, NS]),
            dvec=din(f"dvec_{p}", [128, 16]),
            outwT=din(f"outwT_{p}", [DI, DM]),
        )
    projT = din("projT", [2 * DM, DM])
    projb = din("projb", [128, 8])
    inprojT = din("inprojT", [DM, 3 * DM])
    inprojb = din("inprojb", [128, 24])
    outw2T = din("outw2T", [DM, DM])
    outb2 = din("outb2", [128, 8])
    gateT = din("gateT", [DM, NE])
    gatebr = din("gatebr", [128, NE])
    w1t = din("w1t", [NE, 32, 8, 128, 128])
    b1h = din("b1h", [NE, 128, 32])
    w2t = din("w2t", [NE, 8, 32, 128, 128])
    b2h = din("b2h", [NE, 128, 8])
    ln1g = din("ln1g", [128, 8]); ln1b = din("ln1b", [128, 8])
    ln2g = din("ln2g", [128, 8]); ln2b = din("ln2b", [128, 8])
    ident_in = din("ident_in", [128, 128], F16)
    ones16_in = din("ones16", [128, 1], F16)
    onesrow_in = din("onesrow", [1, 128])
    headsel_in = din("headsel", [128, 8, HD], F16)
    headexp_in = din("headexp", [HD, 8, 128], F16)
    selBC_in = din("selBC", [96, 32, 128], F16)
    sel8_in = din("sel8", [8, NE, 128], F16)
    mrfill_in = din("mrfill", [128, NE])
    cst_in = din("cst", [128, 2])

    out_h = nc.dram_tensor("out", [DM, TOK], F32, kind="ExternalOutput").ap()
    taps = {}
    for t in debug_taps:
        shp = {
            "hn": [DM, L], "xr": [DI, L], "u": [DI, L], "dt": [DI, L],
            "dbc": [96, L], "yact": [DI, L], "g": [DI, L], "o_f": [DM, L],
            "o_b": [DM, L], "x1": [DM, L], "qkv": [3 * DM, TOK],
            "S": [16, 8 * SL], "ctx": [DM, TOK], "x2": [DM, TOK],
            "comb": [8, TOK], "moe": [DM, TOK],
            "crep2": [128, TOK], "h1b": [FF, 2 * CW],
        }[t]
        taps[t] = nc.dram_tensor("tap_" + t, shp, F16, kind="ExternalOutput").ap()

    def scr(name, shape, dt=F16):
        return nc.dram_tensor(name, shape, dt, kind="Internal").ap()

    hnsp = {"f": scr("hnspf", [8, 128, L]), "b": scr("hnspb", [8, 128, L])}
    szsp = scr("szsp", [16, 128, L])
    usp = scr("usp", [16, 128, L])
    dtsp = scr("dtsp", [16, 128, L])
    dusp = scr("dusp", [16, 128, L])
    osp = {"f": scr("ospf", [8, 128, L]), "b": scr("ospb", [8, 128, L])}
    y0sp = scr("y0sp", [8, 128, TOK])
    cc_in = scr("cc_in", [DM, L])
    cc_out = nc.dram_tensor("cc_out", [NB, DM, L], F16, kind="Internal",
                            addr_space="Shared").ap()

    sems = {s: cc(nc.semaphore(s)) for s in SEMS}
    pb = [cc(nc.psum_tensor(f"pb{i}", [128, CW], F32)) for i in range(8)]
    banklast = [None] * 8

    def mmg(bank, mmfn, waits, evac_eng, evacfn, evac_waits=()):
        w = list(waits)
        if banklast[bank] is not None:
            w.append(banklast[bank])
        mv = P.add("tensor", mmfn, waits=w, mark="T")
        ev = P.add(evac_eng, evacfn, waits=[mv, *evac_waits],
                   mark={"vector": "V", "scalar": "S"}[evac_eng])
        banklast[bank] = ev
        return ev

    g_lng = cc(nc.sbuf_tensor("g_lng", [128, 8], F32))
    g_lnb = cc(nc.sbuf_tensor("g_lnb", [128, 8], F32))
    g_projb = cc(nc.sbuf_tensor("g_projb", [128, 8], F32))
    g_ident = cc(nc.sbuf_tensor("g_ident", [128, 128], F16))
    g_ones16 = cc(nc.sbuf_tensor("g_ones16", [128, 1], F16))
    g_onesrow = cc(nc.sbuf_tensor("g_onesrow", [1, 128], F32))
    g_selBC = cc(nc.sbuf_tensor("g_selBC", [96, 32, 128], F16))
    g_conv = cc(nc.sbuf_tensor("g_conv", [128, 16, 4, 128], F16))
    g_convb = {p: cc(nc.sbuf_tensor(f"g_convb_{p}", [128, 16], F32)) for p in ("f", "b")}
    g_dtb = {p: cc(nc.sbuf_tensor(f"g_dtb_{p}", [128, 16], F32)) for p in ("f", "b")}
    g_dvec = {p: cc(nc.sbuf_tensor(f"g_dvec_{p}", [128, 16], F32)) for p in ("f", "b")}
    g_A = {p: cc(nc.sbuf_tensor(f"g_A_{p}", [128, 16, NS], F32)) for p in ("f", "b")}
    g_dtw = cc(nc.sbuf_tensor("g_dtw", [DTR, DI], F16))
    g_xproj = cc(nc.sbuf_tensor("g_xproj", [128, 16, 96], F16))
    g_inprojb = cc(nc.sbuf_tensor("g_inprojb", [128, 24], F32))
    g_outb2 = cc(nc.sbuf_tensor("g_outb2", [128, 8], F32))
    g_gatebr = cc(nc.sbuf_tensor("g_gatebr", [128, NE], F32))
    g_b1 = cc(nc.sbuf_tensor("g_b1", [128, NE, 32], F32))
    g_b2 = cc(nc.sbuf_tensor("g_b2", [128, NE, 8], F32))
    g_ln1g = cc(nc.sbuf_tensor("g_ln1g", [128, 8], F32))
    g_ln1b = cc(nc.sbuf_tensor("g_ln1b", [128, 8], F32))
    g_ln2g = cc(nc.sbuf_tensor("g_ln2g", [128, 8], F32))
    g_ln2b = cc(nc.sbuf_tensor("g_ln2b", [128, 8], F32))
    g_headsel = cc(nc.sbuf_tensor("g_headsel", [128, 8, HD], F16))
    g_headexp = cc(nc.sbuf_tensor("g_headexp", [HD, 8, 128], F16))
    g_sel8 = cc(nc.sbuf_tensor("g_sel8", [8, NE, 128], F16))
    g_mrfill = cc(nc.sbuf_tensor("g_mrfill", [128, NE], F32))
    g_cst = cc(nc.sbuf_tensor("g_cst", [128, 2], F32))

    loads = [
        (g_lng[:], lng), (g_lnb[:], lnb), (g_projb[:], projb),
        (g_onesrow[:], onesrow_in), (g_inprojb[:], inprojb),
        (g_outb2[:], outb2), (g_gatebr[:], gatebr),
        (g_ln1g[:], ln1g), (g_ln1b[:], ln1b), (g_ln2g[:], ln2g),
        (g_ln2b[:], ln2b), (g_mrfill[:], mrfill_in),
        (g_ident[:], ident_in), (g_ones16[:], ones16_in),
        (g_headsel[:], headsel_in), (g_headexp[:], headexp_in),
        (g_selBC[:], selBC_in), (g_sel8[:], sel8_in),
        (g_cst[:], cst_in),
    ]
    for e in range(NE):
        loads += [(g_b1[:, e, :], b1h[e]), (g_b2[:, e, :], b2h[e])]
    for p in ("f", "b"):
        loads += [(g_convb[p][:], br_in[p]["convb"]), (g_dtb[p][:], br_in[p]["dtb"]),
                  (g_dvec[p][:], br_in[p]["dvec"]), (g_A[p][:], br_in[p]["alog"])]
    P.add("sync", lambda s: (nc.partition_id((mybir.EngineType.SP,)),
                             s.nop())[-1])
    for dst, src in loads:
        P.add("sync", lambda s, dst=dst, src=src: s.dma_start(out=dst, in_=src),
              mark="dS")
    b0 = P.bar()
    ae = [P.add("scalar", lambda s, p=p: s.activation(
        out=g_A[p][:], in_=g_A[p][:], func=AF.Exp), waits=b0, mark="S")
        for p in ("f", "b")]
    negA = [P.add("vector", lambda v, p=p: v.tensor_scalar_mul(
        g_A[p][:], g_A[p][:], -1.0), waits=ae, mark="V") for p in ("f", "b")]

    AR = None
    for trysz in (170, 168, 166, 164, 160, 156, 152):
        try:
            AR = Arena(nc, trysz * 1024)
            break
        except AssertionError:
            continue
    assert AR is not None, "could not allocate arena"

    def make_ln(tag, with16):
        lnsq = AR.al(tag + "lnsq", [128, 8, CW], F16)
        lnrow = AR.al(tag + "lnrow", [1, 6 * CW], F32)
        lnrep = AR.al(tag + "lnrep", [128, 2, CW], F32)
        lnnrm = AR.al(tag + "lnnrm", [128, 8, CW], F16)
        ln16 = AR.al(tag + "ln16", [128, 8, CW], F16) if with16 else None
        names = [tag + n for n in ("lnsq", "lnrow", "lnrep", "lnnrm")]
        if with16:
            names.append(tag + "ln16")

        def ln_free():
            AR.fr(*names)

        def emit_ln(nd, xap, x_is_16, sink, pre=()):
            evs = []
            if x_is_16:
                s16 = xap
            else:
                for i in range(nd):
                    evs.append(P.add("vector", lambda v, i=i: v.tensor_copy(
                        out=ln16[:, i, :], in_=xap(i)), waits=pre, mark="V"))
                s16 = lambda i: ln16[:, i, :]
            for i in range(nd):
                evs.append(P.add("scalar", lambda s, i=i: s.activation(
                    out=lnsq[:, i, :], in_=xap(i), func=AF.Square),
                    waits=pre, mark="S"))

            def statf(t, nd=nd):
                for i in range(nd):
                    t.matmul(pb[6][:1, :], g_ones16[:], s16(i),
                             start=(i == 0), stop=(i == nd - 1))
                for i in range(nd):
                    v = t.matmul(pb[7][:1, :], g_ones16[:], lnsq[:, i, :],
                                 start=(i == 0), stop=(i == nd - 1))
                return v
            w = list(evs) + list(pre)
            if banklast[6] is not None:
                w.append(banklast[6])
            if banklast[7] is not None:
                w.append(banklast[7])
            mv = P.add("tensor", statf, waits=w, mark="T")
            r = [lnrow[:, k * CW:(k + 1) * CW] for k in range(6)]
            P.add("vector", lambda v: v.tensor_scalar_mul(
                r[2], pb[6][:1, :], 1.0 / (nd * 128)), waits=[mv])
            P.add("vector", lambda v: v.tensor_scalar_mul(
                r[1], pb[7][:1, :], 1.0 / (nd * 128)))
            P.add("vector", lambda v: v.tensor_mul(r[0], r[2], r[2]))
            e4 = P.add("vector", lambda v: v.tensor_sub(r[3], r[1], r[0]), mark="V")
            P.add("scalar", lambda s: s.activation(out=r[0], in_=r[3], func=AF.Ln,
                                                   bias=g_cst[0:1, 0:1]),
                  waits=[e4])
            e6 = P.add("scalar", lambda s: s.activation(out=r[4], in_=r[0],
                                                        func=AF.Exp,
                                                        scale=g_cst[0:1, 1:2]),
                       mark="S")
            P.add("vector", lambda v: v.tensor_mul(r[5], r[2], r[4]), waits=[e6])
            e8 = P.add("vector", lambda v: v.tensor_scalar_mul(r[5], r[5], -1.0),
                       mark="V")

            def repf(t):
                t.matmul(pb[6][:, :], g_onesrow[:], r[4], start=True, stop=True)
                return t.matmul(pb[7][:, :], g_onesrow[:], r[5], start=True,
                                stop=True)
            mv2 = P.add("tensor", repf, waits=[e8], mark="T")
            ea = P.add("scalar", lambda s: s.copy(out=lnrep[:, 0, :],
                                                  in_=pb[6][:, :]),
                       waits=[mv2], mark="S")
            eb = P.add("scalar", lambda s: s.copy(out=lnrep[:, 1, :],
                                                  in_=pb[7][:, :]), mark="S")
            banklast[6] = eb
            banklast[7] = eb
            outs = []
            for i in range(nd):
                P.add("vector", lambda v, i=i: v.tensor_mul(
                    lnnrm[:, i, :], xap(i), lnrep[:, 0, :]), waits=[ea, eb])
                ev = P.add("vector", lambda v, i=i: v.tensor_add(
                    lnnrm[:, i, :], lnnrm[:, i, :], lnrep[:, 1, :]), mark="V")
                outs += sink(i, lnnrm[:, i, :], [ev])
            return outs
        return emit_ln, ln_free

    def tapdump(name, get_src, ntile, waits):
        if name not in taps:
            return
        for i in range(max(ntile, 1)):
            if ntile > 1:
                d = t128(taps[name])[:, i, :]
            else:
                d = taps[name][:]
            P.add("gpsimd", lambda s, i=i, d=d: s.dma_start(out=d,
                                                            in_=get_src(i)),
                  waits=waits, mark="dG")

    import os as _os
    if _os.environ.get("KN_FAKE1"):
        P.add("gpsimd", lambda g: g.dma_start(out=cc_in, in_=xT),
              waits=P.bar(), mark="dG")
        agev = P.add("gpsimd", lambda g: g.collective_compute(
            "AllGather", ALU.bypass, ins=[cc_in], outs=[cc_out],
            replica_groups=[[0, 1, 2, 3, 4, 5, 6, 7]]), waits=P.bar(), mark="G")
        globals()["_FAKE_AG"] = True
    emit_ln1, lnfree1 = make_ln("p1", True)
    xch = [AR.al(f"xch{i}", [128, 8, CW], F32) for i in range(2)]
    hnc = AR.al("hnc", [128, 8, CW], F16)
    hnRc = AR.al("hnRc", [128, 8, CW], F16)

    prev_sink = P.bar()
    for c in range(CH if not _os.environ.get("KN_FAKE1") else 0):
        lx = P.add("sync", lambda s, c=c: s.dma_start(
            out=xch[c % 2][:], in_=t128(xT)[:, :, c * CW:(c + 1) * CW]),
            waits=prev_sink, mark="dS")

        def sink1(i, t16, deps, c=c):
            e1 = P.add("vector", lambda v, i=i, t16=t16: v.tensor_scalar(
                out=hnc[:, i, :], in0=t16, scalar1=g_lng[:, i:i + 1],
                scalar2=g_lnb[:, i:i + 1], op0=ALU.mult, op1=ALU.add),
                waits=deps, mark="V")
            e2 = P.add("vector", lambda v, i=i: v.tensor_copy(
                out=hnRc[:, i, :], in_=hnc[:, i, :][:, ::-1]), waits=[e1], mark="V")
            e3 = P.add("sync", lambda s, i=i, c=c: s.dma_start(
                out=hnsp["f"][i, :, c * CW:(c + 1) * CW], in_=hnc[:, i, :]),
                waits=[e1], mark="dS")
            e4 = P.add("sync", lambda s, i=i, c=c: s.dma_start(
                out=hnsp["b"][i, :, (CH - 1 - c) * CW:(CH - c) * CW],
                in_=hnRc[:, i, :]), waits=[e2], mark="dS")
            return [e3, e4]

        prev_sink = emit_ln1(8, lambda i, c=c: xch[c % 2][:, i, :], False, sink1,
                             pre=[lx])
    tapdump("hn", lambda i: hnsp["f"][i], 8, P.bar())
    lnfree1()
    AR.fr("xch0", "xch1", "hnc", "hnRc")
    if not want("branchf"):
        return _finish(nc, ctx, P, (sems, taps))

    def branch(p):
        ent = P.bar()
        xrb = AR.al("xrb", [128, 16, 2054], F16)
        inwh = AR.al("inwh", [128, 8, DI], F16)
        hnst = [AR.al(f"hnst{i}", [128, 8, CW], F16) for i in range(2)]
        evh = [AR.al(f"evh{i}", [128, CW], F16) for i in range(2)]

        xr_zero = P.add("vector", lambda v: v.memset(xrb[:], 0.0), waits=ent,
                        mark="V")
        cwld = P.add("gpsimd", lambda g: g.dma_start(
            out=g_conv[:], in_=br_in[p]["convd"]), waits=ent, mark="dG")
        xwld = P.add("gpsimd", lambda g: g.dma_start(
            out=g_xproj[:], in_=br_in[p]["xprojT"]), waits=ent, mark="dG")
        dwld = P.add("gpsimd", lambda g: g.dma_start(
            out=g_dtw[:], in_=br_in[p]["dtwT"]), waits=ent, mark="dG")

        hs_t = hnsp[p].transpose([1, 0, 2])
        evev = {}
        szslot = {}
        for half in range(2):
            wev = P.add("gpsimd", lambda g, half=half: g.dma_start(
                out=inwh[:],
                in_=t128(br_in[p]["inwT"])[:, :, half * DI:(half + 1) * DI]),
                waits=ent + ([evev[(0, 63)]] if half else []), mark="dG")
            hld = [None] * CH
            for c in range(2):
                hld[c] = P.add("sync", lambda s, c=c: s.dma_start(
                    out=hnst[c % 2][:], in_=hs_t[:, :, c * CW:(c + 1) * CW]),
                    waits=[wev] + ent, mark="dS")
            for c in range(CH):
                if c >= 2:
                    hld[c] = P.add("sync", lambda s, c=c: s.dma_start(
                        out=hnst[c % 2][:], in_=hs_t[:, :, c * CW:(c + 1) * CW]),
                        waits=[evev[(half, (c - 1) * 16 + 15)]], mark="dS")
                for fo in range(16):
                    g = (half, c * 16 + fo)
                    gi = c * 16 + fo

                    def mmfn(t, c=c, fo=fo, gi=gi):
                        for kt in range(8):
                            v = t.matmul(pb[gi % 4][:],
                                         inwh[:, kt, fo * 128:(fo + 1) * 128],
                                         hnst[c % 2][:, kt, :],
                                         start=(kt == 0), stop=(kt == 7))
                        return v
                    if half == 0:
                        evev[g] = mmg(gi % 4, mmfn, [wev, hld[c], xr_zero],
                                      "scalar",
                                      lambda s, c=c, fo=fo, gi=gi: s.copy(
                                          out=xrb[:, fo,
                                                  3 + c * CW:3 + (c + 1) * CW],
                                          in_=pb[gi % 4][:]))
                    else:
                        ew = [szslot[gi % 2]] if gi % 2 in szslot else []
                        evev[g] = mmg(gi % 4, mmfn, [wev, hld[c]], "scalar",
                                      lambda s, gi=gi: s.activation(
                                          out=evh[gi % 2][:], in_=pb[gi % 4][:],
                                          func=AF.Silu), evac_waits=ew)
                        szslot[gi % 2] = P.add(
                            "gpsimd", lambda gp, c=c, fo=fo, gi=gi: gp.dma_start(
                                out=szsp[fo, :, c * CW:(c + 1) * CW],
                                in_=evh[gi % 2][:]),
                            waits=[evev[g]], mark="dG")
        if p == "f":
            tapdump("xr", lambda i: xrb[:, i, 3:3 + L], 16, P.bar())

        AR.fr("inwh", "hnst0", "hnst1", "evh0", "evh1")
        ub = AR.al("ub", [128, 16, L], F16)
        uev = {}
        for dt_i in range(16):
            for c in range(CH):
                g = dt_i * CH + c

                def cfn(t, dt_i=dt_i, c=c, g=g):
                    for tap in range(4):
                        v = t.matmul(pb[g % 4][:], g_conv[:, dt_i, tap, :],
                                     xrb[:, dt_i, c * CW + tap:c * CW + tap + CW],
                                     start=(tap == 0), stop=(tap == 3))
                    return v
                w3 = [evev[(0, c * 16 + dt_i)], cwld]
                if c > 0:
                    w3.append(evev[(0, (c - 1) * 16 + dt_i)])
                uev[(dt_i, c)] = mmg(g % 4, cfn, w3, "scalar",
                                     lambda s, dt_i=dt_i, c=c, g=g: s.activation(
                                         out=ub[:, dt_i, c * CW:(c + 1) * CW],
                                         in_=pb[g % 4][:], func=AF.Silu,
                                         bias=g_convb[p][:, dt_i:dt_i + 1]))
        if p == "f":
            tapdump("u", lambda i: ub[:, i, :], 16, P.bar())

        AR.fr("xrb")
        dbc16 = AR.al("dbc16", [96, L], F16)
        dbev = []
        for c in range(CH):
            w4 = [uev[(kt, c)] for kt in range(16)] + [xwld]

            def xpf(t, c=c):
                for kt in range(16):
                    v = t.matmul(pb[4 + c % 2][:96, :], g_xproj[:, kt, :],
                                 ub[:, kt, c * CW:(c + 1) * CW],
                                 start=(kt == 0), stop=(kt == 15))
                return v
            dbev.append(mmg(4 + c % 2, xpf, w4, "scalar",
                            lambda s, c=c: s.copy(
                                out=dbc16[:, c * CW:(c + 1) * CW],
                                in_=pb[4 + c % 2][:96, :])))
        if p == "f":
            tapdump("dbc", lambda i: dbc16[:, :], 1, P.bar())

        dtpg = [AR.al(f"dtpg{i}", [128, L], F16) for i in range(2)]
        e1b = [AR.al(f"e1b{i}", [128, CW], F32) for i in range(2)]
        dtuc = AR.al("dtuc", [128, L], F16)
        spl = {}
        duev = {}
        for dt_i in range(16):
            ch_ev = []
            for c in range(CH):
                g = dt_i * CH + c
                w6 = []
                if dt_i >= 2 and c == 0:
                    w6.append(spl[dt_i - 2])

                def dtev(s, dt_i=dt_i, c=c, g=g):
                    s.activation(out=e1b[g % 2][:], in_=pb[g % 4][:], func=AF.Exp,
                                 bias=g_dtb[p][:, dt_i:dt_i + 1])
                    return s.activation(
                        out=dtpg[dt_i % 2][:, c * CW:(c + 1) * CW],
                        in_=e1b[g % 2][:], func=AF.Ln, bias=1.0)
                ch_ev.append(mmg(g % 4, lambda t, dt_i=dt_i, c=c, g=g: t.matmul(
                    pb[g % 4][:], g_dtw[:, dt_i * 128:(dt_i + 1) * 128],
                    dbc16[:DTR, c * CW:(c + 1) * CW], start=True, stop=True),
                    list(dbev) + [dwld], "scalar", dtev, evac_waits=w6))
            du = P.add("vector", lambda v, dt_i=dt_i: v.tensor_mul(
                dtuc[:], dtpg[dt_i % 2][:], ub[:, dt_i, :]),
                waits=ch_ev + ([duev[dt_i - 1]] if dt_i >= 1 else []), mark="V")
            duev[dt_i] = du
            P.add("sync", lambda s, dt_i=dt_i: s.dma_start(
                out=dtsp[dt_i], in_=dtpg[dt_i % 2][:]), waits=[du], mark="dS")
            P.add("sync", lambda s, dt_i=dt_i: s.dma_start(
                out=usp[dt_i], in_=ub[:, dt_i, :]), waits=[du], mark="dS")
            spl[dt_i] = P.add("sync", lambda s, dt_i=dt_i: s.dma_start(
                out=dusp[dt_i], in_=dtuc[:]), waits=[du], mark="dS")

        AR.fr("ub", "dtpg0", "dtpg1", "e1b0", "e1b1", "dtuc")
        yac = AR.al("yac", [128, 16, L], F16)
        brp = AR.al("brp", [128, 2, 2, L], F16)
        crp = AR.al("crp", [128, 2, 2, L], F16)
        dtt = [AR.al(f"dtt{i}", [128, L], F16) for i in range(2)]
        dut = [AR.al(f"dut{i}", [128, L], F16) for i in range(2)]
        ab = [AR.al(f"ab{i}", [128, L], F16) for i in range(2)]
        bb_ = [AR.al(f"bb{i}", [128, L], F16) for i in range(2)]
        hb = [AR.al(f"hb{i}", [128, L], F16) for i in range(2)]
        tb = [AR.al(f"tb{i}", [128, L], F16) for i in range(2)]
        fence_e = P.bar()

        yinit = {}
        for dt_i in range(16):
            lu = P.add("sync", lambda s, dt_i=dt_i: s.dma_start(
                out=dtt[dt_i % 2][:], in_=usp[dt_i]),
                waits=fence_e + ([yinit[dt_i - 2]] if dt_i >= 2 else []),
                mark="dS")
            yinit[dt_i] = P.add("vector", lambda v, dt_i=dt_i: v.tensor_scalar_mul(
                yac[:, dt_i, :], dtt[dt_i % 2][:], g_dvec[p][:, dt_i:dt_i + 1]),
                waits=[lu], mark="V")
        yi_all = list(yinit.values())

        NGW = 2
        NG = 16 // NGW
        repev = {}
        scan_ev = {}
        yadd_ev = {}
        bmul_ev = {}
        ymul_ev = {}
        aev = {}
        for ng in range(NG):
            for i in range(NGW):
                n = ng * NGW + i
                for bc in range(2):
                    dest = brp if bc == 0 else crp
                    sel = n if bc == 0 else 16 + n
                    last = None
                    for c in range(CH):
                        gg = 4 + (((n * 2 + bc) * CH + c) % 2)
                        w7 = list(dbev) + fence_e
                        if ng >= 2:
                            w7.append(ymul_ev.get((ng - 2, 15, NGW - 1)))
                            w7.append(bmul_ev.get((ng - 2, 15, NGW - 1)))
                        last = mmg(gg, lambda t, sel=sel, c=c, gg=gg: t.matmul(
                            pb[gg][:], g_selBC[:, sel, :],
                            dbc16[:, c * CW:(c + 1) * CW], start=True, stop=True),
                            w7, "scalar",
                            lambda s, dest=dest, ng=ng, i=i, c=c, gg=gg: s.copy(
                                out=dest[:, ng % 2, i, c * CW:(c + 1) * CW],
                                in_=pb[gg][:]))
                    repev[(ng, i, bc)] = last

            for dt_i in range(16):
                it0 = ng * 16 + dt_i
                wl = (list(fence_e) + yi_all) if it0 < 2 else \
                    [aev[((it0 - 2) // 16, (it0 - 2) % 16, NGW - 1)]]
                dl = P.add("sync", lambda s, dt_i=dt_i: s.dma_start(
                    out=dtt[dt_i % 2][:], in_=dtsp[dt_i]), waits=wl, mark="dS")
                wl2 = (list(fence_e) + yi_all) if it0 < 2 else \
                    [bmul_ev[((it0 - 2) // 16, (it0 - 2) % 16, NGW - 1)]]
                dul = P.add("sync", lambda s, dt_i=dt_i: s.dma_start(
                    out=dut[dt_i % 2][:], in_=dusp[dt_i]), waits=wl2, mark="dS")
                for i in range(NGW):
                    n = ng * NGW + i
                    it = it0 * NGW + i
                    wa = [dl] + negA
                    if it >= 2:
                        wa.append(scan_ev[it - 2])
                    aev[(ng, dt_i, i)] = P.add(
                        "scalar", lambda s, dt_i=dt_i, n=n, it=it: s.activation(
                            out=ab[it % 2][:], in_=dtt[dt_i % 2][:], func=AF.Exp,
                            scale=g_A[p][:, dt_i, n:n + 1]), waits=wa, mark="S")
                    wb = [repev[(ng, i, 0)], dul]
                    if it >= 2:
                        wb.append(scan_ev[it - 2])
                    bmul_ev[(ng, dt_i, i)] = P.add(
                        "gpsimd", lambda g, dt_i=dt_i, ng=ng, i=i, it=it:
                        g.tensor_mul(bb_[it % 2][:], dut[dt_i % 2][:],
                                     brp[:, ng % 2, i, :]), waits=wb, mark="G")
                    ws = [aev[(ng, dt_i, i)], bmul_ev[(ng, dt_i, i)]]
                    if it >= 2:
                        ws.append(ymul_ev[it - 2])
                    scan_ev[it] = P.add(
                        "vector", lambda v, it=it: v.tensor_tensor_scan(
                            out=hb[it % 2][:], data0=ab[it % 2][:],
                            data1=bb_[it % 2][:], initial=0.0,
                            op0=ALU.mult, op1=ALU.add), waits=ws, mark="V")
                    wy = [scan_ev[it], repev[(ng, i, 1)]]
                    if it >= 2:
                        wy.append(yadd_ev[it - 2])
                    ymul_ev[it] = ymul_ev[(ng, dt_i, i)] = P.add(
                        "gpsimd", lambda g, ng=ng, i=i, it=it: g.tensor_mul(
                            tb[it % 2][:], hb[it % 2][:], crp[:, ng % 2, i, :]),
                        waits=wy, mark="G")
                    yadd_ev[it] = P.add(
                        "vector", lambda v, dt_i=dt_i, it=it: v.tensor_add(
                            yac[:, dt_i, :], yac[:, dt_i, :], tb[it % 2][:]),
                        waits=[ymul_ev[it], yinit[dt_i]], mark="V")
        if p == "f":
            tapdump("yact", lambda i: yac[:, i, :], 16, P.bar())

        AR.fr("brp", "crp", "dtt0", "dtt1", "dut0", "dut1", "ab0", "ab1",
              "bb0", "bb1", "hb0", "hb1", "tb0", "tb1", "dbc16")
        ow16 = AR.al("ow16", [128, 16, DM], F16)
        szt = [AR.al(f"szt{i}", [128, L], F16) for i in range(2)]
        oev = [AR.al(f"oev{i}", [128, CW], F16) for i in range(2)]
        orv = AR.al("orv", [128, CW], F16)
        fence_f = P.bar()

        gev = {}
        owev = P.add("gpsimd", lambda s: s.dma_start(
            out=ow16, in_=t128(br_in[p]["outwT"])), waits=fence_f, mark="dG")
        for dt_i in range(16):
            ydone = yadd_ev[((NG - 1) * 16 + dt_i) * NGW + NGW - 1]
            szl = P.add("sync", lambda s, dt_i=dt_i: s.dma_start(
                out=szt[dt_i % 2], in_=szsp[dt_i]),
                waits=[ydone] + fence_f +
                ([gev[dt_i - 2]] if dt_i >= 2 else []), mark="dS")
            gev[dt_i] = P.add("vector", lambda v, dt_i=dt_i: v.tensor_mul(
                yac[:, dt_i, :], szt[dt_i % 2], yac[:, dt_i, :]),
                waits=[szl, ydone], mark="V")
        if p == "f":
            tapdump("g", lambda i: yac[:, i, :], 16, P.bar())

        oslot = {}
        for dm in range(8):
            for c in range(CH):
                g = dm * CH + c

                def opf(t, dm=dm, c=c, g=g):
                    for kt in range(16):
                        v = t.matmul(pb[g % 4][:],
                                     ow16[:, kt, dm * 128:(dm + 1) * 128],
                                     yac[:, kt, c * CW:(c + 1) * CW],
                                     start=(kt == 0), stop=(kt == 15))
                    return v
                se = mmg(g % 4, opf, [owev] + [gev[kt] for kt in range(16)],
                         "scalar",
                         lambda s, g=g: s.copy(out=oev[g % 2],
                                               in_=pb[g % 4][:]),
                         evac_waits=[oslot.get(g % 2)])
                if p == "f":
                    oslot[g % 2] = P.add(
                        "sync", lambda s, dm=dm, c=c, g=g: s.dma_start(
                            out=osp["f"][dm, :, c * CW:(c + 1) * CW],
                            in_=oev[g % 2]), waits=[se], mark="dS")
                else:
                    rv = P.add("vector", lambda v, g=g: v.tensor_copy(
                        out=orv, in_=oev[g % 2][:, ::-1]), waits=[se],
                        mark="V")
                    oslot[g % 2] = rv
                    P.add("sync", lambda s, dm=dm, c=c: s.dma_start(
                        out=osp["b"][dm, :, (CH - 1 - c) * CW:(CH - c) * CW],
                        in_=orv), waits=[rv], mark="dS")
        AR.fr("yac", "ow16", "szt0", "szt1", "oev0", "oev1", "orv")

    if _os.environ.get("KN_FAKE1"):
        lnfree1()
        AR.fr("xch0", "xch1", "hnc", "hnRc")
        return _build_phase2(**locals()) if False else None or _phase2_entry()
    branch("f")
    if not want("branchb"):
        tapdump("o_f", lambda i: osp["f"][i], 8, P.bar())
        return _finish(nc, ctx, P, (sems, taps))
    branch("b")
    tapdump("o_f", lambda i: osp["f"][i], 8, P.bar())
    tapdump("o_b", lambda i: osp["b"][i], 8, P.bar())
    if not want("p4"):
        return _finish(nc, ctx, P, (sems, taps))

    emit_ln4, lnfree4 = make_ln("p4", False)
    ofb = AR.al("ofb", [128, 8, L], F16)
    obb = AR.al("obb", [128, 8, L], F16)
    proj16 = AR.al("proj16", [128, 16, DM], F16)
    xc4 = AR.al("xc4", [128, 8, CW], F16)
    r4 = AR.al("r4", [128, 8, CW], F16)
    x1c16 = AR.al("x1c16", [128, 8, CW], F16)
    e1c = [AR.al(f"e1c{i}", [128, CW], F32) for i in range(2)]
    fence4 = P.bar()

    pjev = P.add("gpsimd", lambda g: g.dma_start(out=proj16[:], in_=t128(projT)),
                 waits=fence4, mark="dG")
    old = []
    for dm in range(8):
        old.append(P.add("sync", lambda s, dm=dm: s.dma_start(
            out=ofb[:, dm, :], in_=osp["f"][dm]), waits=fence4, mark="dS"))
        old.append(P.add("sync", lambda s, dm=dm: s.dma_start(
            out=obb[:, dm, :], in_=osp["b"][dm]), waits=fence4, mark="dS"))

    prev4 = []
    for c in range(CH):
        lx = P.add("gpsimd", lambda s, c=c: s.dma_start(
            out=xc4, in_=t128(xT)[:, :, c * CW:(c + 1) * CW]),
            waits=fence4 + prev4, mark="dG")
        pr_ev = {}
        for dm in range(8):
            def pjf(t, dm=dm, c=c):
                for kt in range(16):
                    rhs = (ofb[:, kt, c * CW:(c + 1) * CW] if kt < 8 else
                           obb[:, kt - 8, c * CW:(c + 1) * CW])
                    v = t.matmul(pb[dm % 4][:],
                                 proj16[:, kt, dm * 128:(dm + 1) * 128],
                                 rhs, start=(kt == 0), stop=(kt == 15))
                return v

            def pjev_(v, dm=dm):
                return v.tensor_scalar_add(e1c[dm % 2][:], pb[dm % 4][:],
                                           g_projb[:, dm:dm + 1])
            se = mmg(dm % 4, pjf, [pjev, lx] + old, "vector", pjev_,
                     evac_waits=[pr_ev.get(dm - 2)])
            pr_ev[dm] = P.add("vector",
                              lambda v, dm=dm, c=c: v.scalar_tensor_tensor(
                                  out=r4[:, dm, :], in0=xc4[:, dm, :], scalar=2.0,
                                  in1=e1c[dm % 2][:], op0=ALU.mult, op1=ALU.add),
                              waits=[se], mark="V")

        def sink4(i, t16, deps, c=c):
            e1 = P.add("scalar", lambda s, i=i, t16=t16: s.copy(
                out=x1c16[:, i, :], in_=t16), waits=deps, mark="S")
            e2 = P.add("sync", lambda s, i=i, c=c: s.dma_start(
                out=t128(cc_in)[:, i, c * CW:(c + 1) * CW], in_=x1c16[:, i, :]),
                waits=[e1], mark="dS")
            return [e2]

        prev4 = emit_ln4(8, lambda i: r4[:, i, :], True, sink4,
                         pre=[pr_ev[i] for i in range(8)])
    tapdump("x1", lambda i: t128(cc_in)[:, i, :], 8, P.bar())
    lnfree4()
    if not want("ag"):
        return _finish(nc, ctx, P, (sems, taps))
    AR.fr("ofb", "obb", "proj16", "xc4", "r4", "x1c16", "e1c0", "e1c1")

    agev = P.add("gpsimd", lambda g: g.collective_compute(
        "AllGather", ALU.bypass, ins=[cc_in], outs=[cc_out],
        replica_groups=[[0, 1, 2, 3, 4, 5, 6, 7]]), waits=P.bar(), mark="G")
    if not want("qkv"):
        return _finish(nc, ctx, P, (sems, taps))

    _sl = []

    def ldy0_s(dst, s_i, waits):
        def f(s, dst=dst, s_i=s_i):
            if not _sl:
                pid = nc.partition_id((mybir.EngineType.SP,))
                _sl.append(bass.ts(pid, SL))
            a = cc_out[s_i, :, _sl[0]].rearrange("(f p) q -> p f q", p=128)
            return s.dma_start(out=dst, in_=a)
        return P.add("sync", f, waits=waits, mark="dS")

    y0a = AR.al("y0a", [128, 8, TOK], F16)
    qb = AR.al("qb", [128, 8, TOK], F16)
    kb = AR.al("kb", [128, 8, TOK], F16)
    vb = AR.al("vb", [128, 8, TOK], F16)
    ipw3 = AR.al("ipw3", [128, 8, DM], F16)

    ldy = [ldy0_s(y0a[:, :, s_i * SL:(s_i + 1) * SL], s_i, [agev])
           for s_i in range(8)]
    for ft in range(8):
        P.add("sync", lambda s, ft=ft: s.dma_start(
            out=y0sp[ft], in_=y0a[:, ft, :]), waits=ldy, mark="dS")
    qkev = {}
    ipev = {}
    for third, dest in ((0, qb), (1, kb), (2, vb)):
        ipev[third] = P.add("gpsimd", lambda g, third=third: g.dma_start(
            out=ipw3[:], in_=t128(inprojT)[:, :, third * DM:(third + 1) * DM]),
            waits=[agev] + ([qkev[(third - 1, 31)]] if third else []), mark="dG")
        for fo in range(8):
            for c in range(CH):
                g = fo * CH + c

                def qf(t, fo=fo, c=c, g=g):
                    for kt in range(8):
                        v = t.matmul(pb[g % 4][:],
                                     ipw3[:, kt, fo * 128:(fo + 1) * 128],
                                     y0a[:, kt, c * CW:(c + 1) * CW],
                                     start=(kt == 0), stop=(kt == 7))
                    return v
                qkev[(third, g)] = mmg(
                    g % 4, qf, [ipev[third]] + ldy, "vector",
                    lambda v, third=third, dest=dest, fo=fo, c=c, g=g:
                    v.tensor_scalar_add(dest[:, fo, c * CW:(c + 1) * CW],
                                        pb[g % 4][:],
                                        g_inprojb[:, third * 8 + fo:
                                                  third * 8 + fo + 1]))
    if "qkv" in taps:
        tapdump("qkv", lambda i: (qb, kb, vb)[i // 8][:, i % 8, :], 24, P.bar())
    if not want("attn"):
        return _finish(nc, ctx, P, (sems, taps))

    AR.fr("y0a", "ipw3")
    mbb = AR.al("mbb", [128, 2, 8, SL], F16)
    Ss = AR.al("Ss", [16, 8 * SL], F32)
    S2s = AR.al("S2s", [16, 8 * SL], F32)
    Mrw = AR.al("Mrw", [16, SL], F32)
    ATs = AR.al("ATs", [16, 8 * SL], F16)
    ctxb = AR.al("ctxb", [128, 8, TOK], F16)
    ctmp = [AR.al(f"ctmp{i}", [128, SL], F16) for i in range(2)]

    import os as _os2
    SLIM = int(_os2.environ.get("KN_SLIM", "8"))
    NOSM = _os2.environ.get("KN_NOSM")
    NOCTX = _os2.environ.get("KN_NOCTX")
    TLIM = int(_os2.environ.get("KN_TLIM", "8"))
    NOMM = _os2.environ.get("KN_NOMM")
    cxev = {}
    sft_prev = None
    for s_i in range(SLIM):
        sev = {}
        for t_i in range(TLIM):
            pi = s_i * 8 + t_i
            mev = None
            for ft in range(8):
                w11 = [qkev[(0, ft * CH + s_i // 2)],
                       qkev[(1, ft * CH + t_i // 2)]]
                if pi >= 2:
                    w11.append(sev.get(t_i - 2) if t_i >= 2 else
                               cxev.get((s_i - 1, 7)))
                mev = P.add("vector", lambda v, s_i=s_i, t_i=t_i, ft=ft, pi=pi:
                            v.tensor_mul(
                                mbb[:, pi % 2, ft, :],
                                qb[:, ft, s_i * SL:(s_i + 1) * SL],
                                kb[:, ft, t_i * SL:(t_i + 1) * SL]),
                            waits=w11, mark="V" if ft == 7 else None)

            def smf(t, pi=pi):
                _sb2 = 4
                for kt in range(8):
                    v = t.matmul(pb[_sb2][:16, :SL], g_headsel[:, kt, :],
                                 mbb[:, pi % 2, kt, :],
                                 start=(kt == 0), stop=(kt == 7))
                return v
            if NOMM:
                sev[t_i] = mev
                continue
            _sb = 4
            _ee = "vector" if _os.environ.get("KN_VEVAC") else "scalar"
            def _sevac(e, t_i=t_i, _sb=_sb):
                if _os.environ.get("KN_VEVAC"):
                    return e.tensor_copy(out=Ss[:, t_i * SL:(t_i + 1) * SL],
                                         in_=pb[_sb][:16, :SL])
                return e.copy(out=Ss[:, t_i * SL:(t_i + 1) * SL],
                              in_=pb[_sb][:16, :SL])
            sev[t_i] = mmg(_sb, smf, [mev], _ee, _sevac,
                           evac_waits=[sft_prev])
        if s_i == 0 and not NOMM and TLIM == 8:
            tapdump("S", lambda i: Ss, 1, [sev[7]])
        if NOSM:
            sft_prev = sev[TLIM - 1]
            continue
        em = P.add("vector", lambda v: v.tensor_reduce(
            out=Mrw.rearrange("h (q o) -> h q o", o=1),
            in_=Ss.rearrange("h (t q) -> h t q", t=8).transpose([0, 2, 1]),
            axis=mybir.AxisListType.X, op=ALU.max),
            waits=[sev[t_] for t_ in range(8)], mark="V")
        es = P.add("vector", lambda v: v.tensor_sub(
            S2s.rearrange("h (t q) -> h t q", t=8),
            Ss.rearrange("h (t q) -> h t q", t=8),
            Mrw.rearrange("h (o q) -> h o q", o=1).broadcast_to((16, 8, SL))),
            waits=[em], mark="V")
        ee = P.add("scalar", lambda s: s.activation(out=S2s, in_=S2s,
                                                    func=AF.Exp),
                   waits=[es], mark="S")
        e_sum = P.add("vector", lambda v: v.tensor_reduce(
            out=Mrw.rearrange("h (q o) -> h q o", o=1),
            in_=S2s.rearrange("h (t q) -> h t q", t=8).transpose([0, 2, 1]),
            axis=mybir.AxisListType.X, op=ALU.add), waits=[ee], mark="V")
        e_rcp = P.add("vector", lambda v: v.reciprocal(out=Mrw, in_=Mrw),
                      waits=[e_sum], mark="V")
        sft_prev = P.add("vector", lambda v: v.tensor_mul(
            ATs.rearrange("h (t q) -> h t q", t=8),
            S2s.rearrange("h (t q) -> h t q", t=8),
            Mrw.rearrange("h (o q) -> h o q", o=1).broadcast_to((16, 8, SL))),
            waits=[e_rcp], mark="V")

        if NOCTX:
            continue
        for ft in range(8):
            for t_i in range(8):
                g = (s_i * 8 + ft) * 8 + t_i

                def xpf2(t, t_i=t_i, ft=ft, g=g):
                    return t.matmul(pb[4 + g % 2][:, :SL], g_headexp[:, ft, :],
                                    ATs[:, t_i * SL:(t_i + 1) * SL],
                                    start=True, stop=True)

                def cev(v, s_i=s_i, t_i=t_i, ft=ft, g=g):
                    v.tensor_mul(ctmp[g % 2], pb[4 + g % 2][:, :SL],
                                 vb[:, ft, t_i * SL:(t_i + 1) * SL])
                    if t_i == 0:
                        return v.tensor_copy(
                            out=ctxb[:, ft, s_i * SL:(s_i + 1) * SL],
                            in_=ctmp[g % 2])
                    return v.tensor_add(
                        ctxb[:, ft, s_i * SL:(s_i + 1) * SL],
                        ctxb[:, ft, s_i * SL:(s_i + 1) * SL], ctmp[g % 2])
                cxev[(s_i, ft)] = mmg(4 + g % 2, xpf2, [sft_prev], "vector", cev)
    ctxdone = ([cxev[(SLIM - 1, ft)] for ft in range(8)]
               if cxev else [sft_prev])
    tapdump("ctx", lambda i: ctxb[:, i, :], 8, P.bar())
    AR.fr("qb", "kb", "vb", "mbb", "Ss", "S2s", "Mrw", "ATs", "ctmp0", "ctmp1")
    if not want("ln1"):
        return _finish(nc, ctx, P, (sems, taps))

    emit_ln2, lnfree2 = make_ln("l1", False)
    ow2 = AR.al("ow2", [128, 8, DM], F16)
    x2b = AR.al("x2b", [128, 8, TOK], F16)
    y0c = AR.al("y0c", [128, 8, CW], F16)
    r2b = AR.al("r2b", [128, 8, CW], F16)
    aot = [AR.al(f"aot{i}", [128, CW], F32) for i in range(2)]
    fence2c = P.bar()

    owev2 = P.add("gpsimd", lambda g: g.dma_start(out=ow2[:], in_=t128(outw2T)),
                  waits=fence2c, mark="dG")
    x2done = []
    prevx2 = []
    for c in range(CH):
        y0l = [P.add("sync", lambda s, ft=ft, c=c: s.dma_start(
            out=y0c[:, ft, :], in_=y0sp[ft, :, c * CW:(c + 1) * CW]),
            waits=fence2c + prevx2, mark="dS") for ft in range(8)]
        pr_ev = {}
        for dm in range(8):
            def aof(t, dm=dm, c=c):
                for kt in range(8):
                    v = t.matmul(pb[dm % 4][:],
                                 ow2[:, kt, dm * 128:(dm + 1) * 128],
                                 ctxb[:, kt, c * CW:(c + 1) * CW],
                                 start=(kt == 0), stop=(kt == 7))
                return v
            se = mmg(dm % 4, aof, [owev2] + ctxdone, "vector",
                     lambda v, dm=dm: v.tensor_scalar_add(
                         aot[dm % 2], pb[dm % 4][:], g_outb2[:, dm:dm + 1]),
                     evac_waits=[pr_ev.get(dm - 2)])
            pr_ev[dm] = P.add("vector", lambda v, dm=dm, c=c: v.tensor_add(
                r2b[:, dm, :], aot[dm % 2], y0c[:, dm, :]),
                waits=[se, y0l[dm]], mark="V")

        def sinkx2(i, t16, deps, c=c):
            return [P.add("vector", lambda v, i=i, t16=t16, c=c: v.tensor_scalar(
                out=x2b[:, i, c * CW:(c + 1) * CW], in0=t16,
                scalar1=g_ln1g[:, i:i + 1], scalar2=g_ln1b[:, i:i + 1],
                op0=ALU.mult, op1=ALU.add),
                waits=deps, mark="V")]

        prevx2 = emit_ln2(8, lambda i: r2b[:, i, :], True, sinkx2,
                          pre=[pr_ev[i] for i in range(8)])
        x2done += prevx2
    tapdump("x2", lambda i: x2b[:, i, :], 8, P.bar())
    lnfree2()
    if not want("gate"):
        return _finish(nc, ctx, P, (sems, taps))
    AR.fr("ctxb", "ow2", "y0c", "r2b", "aot0", "aot1")

    gtw = AR.al("gtw", [128, 8, NE], F16)
    glog = AR.al("glog", [128, 16, NE], F32)
    gprb = AR.al("gprb", [128, 16, NE], F32)
    gm8 = AR.al("gm8", [128, 16, NE], F32)
    gd = AR.al("gd", [128, 16, NE], F32)
    geq = AR.al("geq", [128, 16, NE], F32)
    gmask = AR.al("gmask", [128, 16, NE], F32)
    gwv = AR.al("gwv", [128, 16, NE], F16)
    grm = AR.al("grm", [128, 16], F32)
    gden = AR.al("gden", [128, 16], F32)
    gnta = AR.al("gnta", [128, 16], F32)
    gnt = AR.al("gnt", [128, 16], F32)
    grec = AR.al("grec", [128, 16], F32)
    combT = AR.al("combT", [8, TOK], F16)
    fence2d = P.bar()

    gwev = P.add("gpsimd", lambda g: g.dma_start(out=gtw[:], in_=t128(gateT)),
                 waits=fence2d, mark="dG")
    ge = {}
    for tt in range(16):
        def gf(t, tt=tt):
            for kt in range(8):
                v = t.matmul(pb[6][:, :NE], x2b[:, kt, tt * 128:(tt + 1) * 128],
                             gtw[:, kt, :], start=(kt == 0), stop=(kt == 7))
            return v

        def gevac(v, tt=tt):
            return v.tensor_add(glog[:, tt, :], pb[6][:, :NE], g_gatebr[:])
        ge[1, tt] = mmg(6, gf, [gwev] + x2done + fence2d, "vector", gevac)
    for tt in range(16):
        ge[2, tt] = P.add("vector", lambda v, tt=tt: v.tensor_reduce(
            out=grm[:, tt:tt + 1], in_=glog[:, tt, :], axis=mybir.AxisListType.X,
            op=ALU.max), waits=[ge[1, tt]], mark="V")
    for tt in range(16):
        ge[3, tt] = P.add("vector", lambda v, tt=tt: v.tensor_tensor(
            out=gd[:, tt, :], in0=glog[:, tt, :],
            in1=grm[:, tt:tt + 1].broadcast_to((128, NE)),
            op=ALU.subtract), waits=[ge[2, tt]], mark="V")
    for tt in range(16):
        ge[4, tt] = P.add("scalar", lambda s, tt=tt: s.activation(
            out=gprb[:, tt, :], in_=gd[:, tt, :], func=AF.Exp),
            waits=[ge[3, tt]], mark="S")
    for tt in range(16):
        ge[5, tt] = P.add("vector", lambda v, tt=tt: v.max(
            out=gm8[:, tt, :], in_=gprb[:, tt, :]), waits=[ge[4, tt]], mark="V")
    for tt in range(16):
        ge[6, tt] = P.add("vector", lambda v, tt=tt: v.tensor_add(
            gden[:, tt:tt + 1], gm8[:, tt, 0:1], gm8[:, tt, 1:2]),
            waits=[ge[5, tt]], mark="V")
    for tt in range(16):
        ge[7, tt] = P.add("vector", lambda v, tt=tt: v.tensor_add(
            gnta[:, tt:tt + 1], gm8[:, tt, 1:2], gm8[:, tt, 2:3]),
            waits=[ge[5, tt]], mark="V")
    for tt in range(16):
        ge[8, tt] = P.add("vector", lambda v, tt=tt: v.tensor_scalar(
            out=gnt[:, tt:tt + 1], in0=gnta[:, tt:tt + 1], scalar1=-0.5,
            scalar2=None, op0=ALU.mult), waits=[ge[7, tt]], mark="V")
    for tt in range(16):
        ge[9, tt] = P.add("vector", lambda v, tt=tt: v.tensor_tensor(
            out=gd[:, tt, :], in0=gprb[:, tt, :],
            in1=gnt[:, tt:tt + 1].broadcast_to((128, NE)), op=ALU.add),
            waits=[ge[8, tt], ge[4, tt]], mark="V")
    for tt in range(16):
        ge[10, tt] = P.add("scalar", lambda s, tt=tt: s.activation(
            out=geq[:, tt, :], in_=gd[:, tt, :], func=AF.Sign),
            waits=[ge[9, tt]], mark="S")
    for tt in range(16):
        ge[11, tt] = P.add("vector", lambda v, tt=tt: v.reciprocal(
            out=grec[:, tt:tt + 1], in_=gden[:, tt:tt + 1]),
            waits=[ge[6, tt]], mark="V")
    for tt in range(16):
        ge[12, tt] = P.add("vector", lambda v, tt=tt: v.tensor_scalar(
            out=gmask[:, tt, :], in0=geq[:, tt, :], scalar1=0.5, scalar2=0.5,
            op0=ALU.mult, op1=ALU.add), waits=[ge[10, tt]], mark="V")
    for tt in range(16):
        ge[13, tt] = P.add("vector", lambda v, tt=tt: v.tensor_mul(
            gd[:, tt, :], gprb[:, tt, :], gmask[:, tt, :]),
            waits=[ge[12, tt], ge[10, tt]], mark="V")
    for tt in range(16):
        ge[14, tt] = P.add("vector", lambda v, tt=tt: v.tensor_tensor(
            out=gwv[:, tt, :], in0=gd[:, tt, :],
            in1=grec[:, tt:tt + 1].broadcast_to((128, NE)), op=ALU.mult),
            waits=[ge[13, tt], ge[11, tt]], mark="V")
    combev = []
    for tt in range(16):
        w15 = [ge[14, tt]]
        if banklast[7] is not None:
            w15.append(banklast[7])
        mv2 = P.add("tensor", lambda t, tt=tt: t.transpose(
            pb[7][:].bitcast(F16)[:NE, :128], gwv[:, tt, :], g_ident[:]),
            waits=w15, mark="T")
        ce = P.add("scalar", lambda s, tt=tt: s.copy(
            out=combT[:, tt * 128:(tt + 1) * 128],
            in_=pb[7][:].bitcast(F16)[:NE, :128]),
            waits=[mv2], mark="S")
        banklast[7] = ce
        combev.append(ce)
    tapdump("comb", lambda i: combT, 1, P.bar())
    if not want("moe"):
        return _finish(nc, ctx, P, (sems, taps))

    AR.fr("gtw", "glog", "gprb", "gm8", "gd", "geq", "gmask", "gwv",
          "grm", "gden", "gnta", "gnt", "grec")
    moeb = AR.al("moeb", [128, 8, TOK], F16)
    h1b = AR.al("h1b", [128, 32, 2 * CW], F16)
    w1b = [AR.al(f"w1b{i}", [128, 8, 128], F16) for i in range(3)]
    w2b = [AR.al(f"w2b{i}", [128, 32, 128], F16) for i in range(2)]
    crep2 = AR.al("crep2", [128, TOK], F16)
    hrel = [AR.al(f"hrel{i}", [128, CW], F16) for i in range(2)]
    hrel2 = [AR.al(f"hrel2{i}", [128, CW], F16) for i in range(2)]
    fence2e = P.bar()

    mz = P.add("vector", lambda v: v.memset(moeb, 0.0), waits=fence2e,
               mark="V")
    w1ld = {}
    w2ld = {}
    h1cons = {}
    h2cons = {}
    moe_last = [mz]
    crep_done = None
    for e in range(NE):
        crev = None
        for c in range(CH):
            crev = mmg(6, lambda t, e=e, c=c: t.matmul(
                pb[6][:, :], g_sel8[:, e, :], combT[:, c * CW:(c + 1) * CW],
                start=True, stop=True),
                [combev[-1], crep_done] + fence2e, "scalar",
                lambda s, c=c: s.copy(out=crep2[:, c * CW:(c + 1) * CW],
                                      in_=pb[6][:, :]))
        crep_ready = crev
        hrel_last = {}
        hrel2_last = {}
        h1done = {}
        for half in range(2):
            for ft in range(32):
                li = (e * 2 + half) * 32 + ft
                wld = []
                if li >= 3:
                    wld.append(w1ld.get(li - 3))
                    wld.append(h1cons.get(li - 3))
                w1ld[li] = P.add("gpsimd", lambda g, e=e, ft=ft, li=li:
                                 g.dma_start(out=w1b[li % 3],
                                             in_=w1t[e, ft].transpose([1, 0, 2])),
                                 waits=wld + fence2e, mark="dG")
                for chh in range(2):
                    c = half * 2 + chh
                    g = ft * 2 + chh

                    def h1f(t, c=c, li=li, g=g):
                        for kt in range(8):
                            v = t.matmul(pb[g % 2][:], w1b[li % 3][:, kt, :],
                                         x2b[:, kt, c * CW:(c + 1) * CW],
                                         start=(kt == 0), stop=(kt == 7))
                        return v

                    def h1ev(s, e=e, ft=ft, chh=chh, g=g):
                        return s.activation(
                            out=h1b[:, ft, chh * CW:(chh + 1) * CW],
                            in_=pb[g % 2][:], func=AF.Relu,
                            bias=g_b1[:, e, ft:ft + 1])
                    h1done[(half, ft)] = h1cons[li] = mmg(
                        g % 2, h1f, [w1ld[li], crep_ready] + x2done,
                        "scalar", h1ev)
            for dm in range(8):
                li2 = (e * 2 + half) * 8 + dm
                wld2 = []
                if li2 >= 2:
                    wld2.append(w2ld.get(li2 - 2))
                    wld2.append(h2cons.get(li2 - 2))
                w2ld[li2] = P.add("gpsimd", lambda g, e=e, dm=dm, li2=li2:
                                  g.dma_start(out=w2b[li2 % 2],
                                              in_=w2t[e, dm].transpose([1, 0, 2])),
                                  waits=wld2 + fence2e, mark="dG")
                for chh in range(2):
                    c = half * 2 + chh
                    g = dm * 2 + chh

                    def h2f(t, chh=chh, li2=li2, g=g):
                        for kt in range(32):
                            v = t.matmul(pb[2 + g % 2][:], w2b[li2 % 2][:, kt, :],
                                         h1b[:, kt, chh * CW:(chh + 1) * CW],
                                         start=(kt == 0), stop=(kt == 31))
                        return v

                    def h2ev(v, e=e, dm=dm, g=g):
                        return v.tensor_scalar_add(hrel[g % 2],
                                                   pb[2 + g % 2][:],
                                                   g_b2[:, e, dm:dm + 1])
                    hw = [w2ld[li2]] + [h1done[(half, ftt)] for ftt in range(32)]
                    se = mmg(2 + g % 2, h2f, hw, "vector", h2ev,
                             evac_waits=[hrel_last.get(g % 2), crep_ready])
                    m1 = hrel_last[g % 2] = P.add(
                        "vector", lambda v, c=c, g=g: v.tensor_mul(
                            hrel2[g % 2], hrel[g % 2],
                            crep2[:, c * CW:(c + 1) * CW]),
                        waits=[se, hrel2_last.get(g % 2)], mark="V")
                    moe_last.append(P.add(
                        "vector", lambda v, dm=dm, c=c, g=g: v.tensor_add(
                            moeb[:, dm, c * CW:(c + 1) * CW],
                            moeb[:, dm, c * CW:(c + 1) * CW], hrel2[g % 2]),
                        waits=[m1, mz], mark="V"))
                    hrel2_last[g % 2] = moe_last[-1]
                    h2cons[li2] = se
        crep_done = moe_last[-1]
    tapdump("crep2", lambda i: crep2[:, :], 1, P.bar())
    tapdump("h1b", lambda i: h1b[:, i, :], 32, P.bar())
    tapdump("moe", lambda i: moeb[:, i, :], 8, P.bar())
    if not want("full"):
        return _finish(nc, ctx, P, (sems, taps))
    AR.fr("w1b0", "w1b1", "w1b2", "w2b0", "w2b1", "h1b", "crep2",
          "hrel0", "hrel1", "hrel20", "hrel21", "combT")

    emit_ln5, lnfree5 = make_ln("f5", False)
    rr16 = AR.al("rr16", [128, 8, CW], F16)
    x3c = AR.al("x3c", [128, 8, CW], F16)
    y0d = AR.al("y0d", [128, 8, CW], F16)
    outc = AR.al("outc", [128, 8, CW], F32)
    fence2f = P.bar()

    st_ev = []
    prevo = []
    for c in range(CH):
        y0l = [P.add("sync", lambda s, ft=ft, c=c: s.dma_start(
            out=y0d[:, ft, :], in_=y0sp[ft, :, c * CW:(c + 1) * CW]),
            waits=fence2f + prevo, mark="dS") for ft in range(8)]
        r2ev = []
        for i in range(8):
            r2ev.append(P.add("vector", lambda v, i=i, c=c: v.tensor_add(
                rr16[:, i, :], x2b[:, i, c * CW:(c + 1) * CW],
                moeb[:, i, c * CW:(c + 1) * CW]),
                waits=fence2f + prevo, mark="V"))

        def sink_ln2(i, t16, deps, c=c):
            return [P.add("vector", lambda v, i=i, t16=t16: v.tensor_scalar(
                out=x3c[:, i, :], in0=t16, scalar1=g_ln2g[:, i:i + 1],
                scalar2=g_ln2b[:, i:i + 1], op0=ALU.mult, op1=ALU.add),
                waits=deps, mark="V")]

        x3done = emit_ln5(8, lambda i: rr16[:, i, :], True, sink_ln2, pre=r2ev)
        r3ev = []
        for i in range(8):
            r3ev.append(P.add("vector", lambda v, i=i, c=c: v.tensor_add(
                rr16[:, i, :], x3c[:, i, :], y0d[:, i, :]),
                waits=x3done + [y0l[i]], mark="V"))

        def sink_out(i, t16, deps, c=c):
            e1 = P.add("scalar", lambda s, i=i, t16=t16: s.copy(
                out=outc[:, i, :], in_=t16), waits=deps, mark="S")
            e2 = P.add("sync", lambda s, i=i, c=c: s.dma_start(
                out=t128(out_h)[:, i, c * CW:(c + 1) * CW], in_=outc[:, i, :]),
                waits=[e1], mark="dS")
            return [e2]

        prevo = emit_ln5(8, lambda i: rr16[:, i, :], True, sink_out, pre=r3ev)
        st_ev += prevo

    P.add("sync", lambda s: s.nop(), waits=st_ev)

    return _finish(nc, ctx, P, block_sems_taps=(sems, taps))


def host_layout(ins):
    f = np.float32
    A = {}

    def col8(v):
        return np.ascontiguousarray(v.reshape(8, 128).T.astype(f))

    def col16(v):
        return np.ascontiguousarray(v.reshape(16, 128).T.astype(f))

    A["lng"] = col8(ins["L0_attn_ln_g"]); A["lnb"] = col8(ins["L0_attn_ln_b"])
    for p, pf in (("f", "fwd"), ("b", "bwd")):
        A[f"inwT_{p}"] = np.ascontiguousarray(ins[f"L0_attn_{pf}_in_w"].T)
        cw = ins[f"L0_attn_{pf}_conv_w"].reshape(16, 128, 4)
        cd = np.zeros((128, 16, 4, 128), f)
        r = np.arange(128)
        cd[r, :, :, r] = cw.transpose(1, 0, 2)[r]
        A[f"convd_{p}"] = cd
        A[f"convb_{p}"] = col16(ins[f"L0_attn_{pf}_conv_b"])
        A[f"xprojT_{p}"] = np.ascontiguousarray(
            ins[f"L0_attn_{pf}_xproj_w"].T.reshape(16, 128, 96)
            .transpose(1, 0, 2))
        A[f"dtwT_{p}"] = np.ascontiguousarray(ins[f"L0_attn_{pf}_dt_w"].T)
        A[f"dtb_{p}"] = col16(ins[f"L0_attn_{pf}_dt_b"])
        A[f"alog_{p}"] = np.ascontiguousarray(
            ins[f"L0_attn_{pf}_A_log"].reshape(16, 128, NS).transpose(1, 0, 2))
        A[f"dvec_{p}"] = col16(ins[f"L0_attn_{pf}_D"])
        A[f"outwT_{p}"] = np.ascontiguousarray(ins[f"L0_attn_{pf}_out_w"].T)
    A["projT"] = np.ascontiguousarray(ins["L0_attn_proj_w"].T)
    A["projb"] = col8(ins["L0_attn_proj_b"])
    A["inprojT"] = np.ascontiguousarray(ins["L0_ffn_in_proj_w"].T)
    A["inprojb"] = np.ascontiguousarray(
        ins["L0_ffn_in_proj_b"].reshape(24, 128).T.astype(f))
    A["outw2T"] = np.ascontiguousarray(ins["L0_ffn_out_w"].T)
    A["outb2"] = col8(ins["L0_ffn_out_b"])
    A["gateT"] = np.ascontiguousarray(ins["L0_ffn_gate_w"].T)
    A["gatebr"] = np.ascontiguousarray(
        np.broadcast_to(ins["L0_ffn_gate_b"], (128, NE)).astype(f))
    A["w1t"] = np.ascontiguousarray(
        ins["L0_ffn_w1"].transpose(0, 2, 1).reshape(NE, 8, 128, 32, 128)
        .transpose(0, 3, 1, 2, 4))
    A["b1h"] = np.ascontiguousarray(
        ins["L0_ffn_b1"].reshape(NE, 32, 128).transpose(0, 2, 1).astype(f))
    A["w2t"] = np.ascontiguousarray(
        ins["L0_ffn_w2"].transpose(0, 2, 1).reshape(NE, 32, 128, 8, 128)
        .transpose(0, 3, 1, 2, 4))
    A["b2h"] = np.ascontiguousarray(
        ins["L0_ffn_b2"].reshape(NE, 8, 128).transpose(0, 2, 1).astype(f))
    A["ln1g"] = col8(ins["L0_ffn_ln1_g"]); A["ln1b"] = col8(ins["L0_ffn_ln1_b"])
    A["ln2g"] = col8(ins["L0_ffn_ln2_g"]); A["ln2b"] = col8(ins["L0_ffn_ln2_b"])

    h = np.float16
    A["ident_in"] = np.eye(128, dtype=h)
    A["ones16"] = np.ones((128, 1), h)
    A["onesrow"] = np.ones((1, 128), f)
    hsel = np.zeros((128, 8, HD), h)
    for kt in range(8):
        for rr in range(128):
            hsel[rr, kt, (kt * 128 + rr) // DH] = 0.125
    A["headsel"] = hsel
    hexp = np.zeros((HD, 8, 128), h)
    for kt in range(8):
        for col in range(128):
            hexp[(kt * 128 + col) // DH, kt, col] = 1.0
    A["headexp"] = hexp
    sbc = np.zeros((96, 32, 128), h)
    for n in range(16):
        sbc[64 + n, n, :] = 1.0
        sbc[80 + n, 16 + n, :] = 1.0
    A["selBC"] = sbc
    s8 = np.zeros((8, NE, 128), h)
    for e in range(NE):
        s8[e, e, :] = 1.0
    A["sel8"] = s8
    A["mrfill"] = np.full((128, NE), 2.0, f)
    A["cst"] = np.ascontiguousarray(
        np.broadcast_to(np.array([1e-5, -0.5], f), (128, 2)))
    return A


_CACHE = {}


def kernel(**inputs):
    ins = {k: np.asarray(v) for k, v in inputs.items()}
    if "nc" not in _CACHE:
        _CACHE["nc"], _CACHE["taps"] = build()
    nc = _CACHE["nc"]
    shared = host_layout(ins)
    x = ins["L0_attn_x"]
    in_maps = []
    for b in range(NB):
        m = dict(shared)
        m["xT"] = np.ascontiguousarray(x[b].T)
        in_maps.append(m)
    res = run_bass_kernel_spmd(nc, in_maps, core_ids=list(range(NB)))
    out = np.empty((NB, L, DM), np.float32)
    for j in range(NB):
        o = np.asarray(res.results[j]["out"])
        out[:, SL * j:SL * (j + 1), :] = o.reshape(DM, 8, SL).transpose(1, 2, 0)
    return out
